# version 21
# baseline (speedup 1.0000x reference)
import numpy as np
import ml_dtypes
import concourse.bass as bass
import concourse.mybir as mybir
from concourse.bass_utils import run_bass_kernel_spmd
from contextlib import ExitStack

F32 = mybir.dt.float32
BF16 = mybir.dt.bfloat16
I8 = mybir.dt.int8
I32 = mybir.dt.int32
AF = mybir.ActivationFunctionType
ALU = mybir.AluOpType
AX = mybir.AxisListType
bf = ml_dtypes.bfloat16

D = 1024
T = 2048
NSEQ = 2
H = 16
HD = 64
KC = 8
NTG = 4
NTC = 16
NSC = 16
EPS_LN = 1e-5
EPS_Q = 1e-6
VW = 1040

ENG_SEM = {"sync": None, "tensor": "pe", "vector": "dve", "scalar": "act", "gpsimd": "pool"}


class Sched:

    SEMS = ("set", "ws0", "ws1", "xs0", "xs1", "xr0", "xr1", "b0", "b1",
            "b2", "b3", "st", "pe", "act", "dve", "pool")

    def __init__(self):
        self.ops = {e: [] for e in ("sync", "tensor", "vector", "scalar", "gpsimd")}
        self.t = {s: 0 for s in self.SEMS}

    def emit(self, eng, builder, waits=(), sem=None, inc=None, nosync=False):
        if sem is None:
            sem = ENG_SEM[eng]
        if inc is None:
            inc = 1 if sem in ("pe", "act", "dve", "pool") else 16
        w = [(s, v) for (s, v) in waits if v and v > 0]
        self.ops[eng].append((builder, w, sem, inc, nosync))
        self.t[sem] += inc
        return self.t[sem]


def build_program():
    nc = bass.Bass("TRN2", target_bir_lowering=False, debug=False)

    x_fm = nc.dram_tensor("x_fm", [NSEQ, D, T], F32, kind="ExternalInput").ap()
    x_tm = nc.dram_tensor("x_tm", [NSEQ, T, D], F32, kind="ExternalInput").ap()
    wq = nc.dram_tensor("wq", [D, D], F32, kind="ExternalInput").ap()
    wk = nc.dram_tensor("wk", [D, D], F32, kind="ExternalInput").ap()
    wv = nc.dram_tensor("wv", [D, D], F32, kind="ExternalInput").ap()
    wu = nc.dram_tensor("wu", [D, D], F32, kind="ExternalInput").ap()
    w8 = nc.dram_tensor("w8", [D, D], I8, kind="ExternalInput").ap()
    bu_b = nc.dram_tensor("bu_b", [128, D], F32, kind="ExternalInput").ap()
    ws_b = nc.dram_tensor("ws_b", [128, D], F32, kind="ExternalInput").ap()
    bias_b = nc.dram_tensor("bias_b", [128, D], F32, kind="ExternalInput").ap()
    ident = nc.dram_tensor("ident", [128, 128], BF16, kind="ExternalInput").ap()
    out = nc.dram_tensor("out", [NSEQ, T, D], F32, kind="ExternalOutput").ap()

    with ExitStack() as ctx:
        E = ctx.enter_context

        wslot = [E(nc.sbuf_tensor(f"wslot{i}", [128, KC * D], BF16)) for i in range(2)]
        stage = [E(nc.sbuf_tensor(f"stage{i}", [128, 1024], F32)) for i in range(2)]
        xres = [E(nc.sbuf_tensor(f"xres{i}", [128, D], F32)) for i in range(2)]
        arena = E(nc.sbuf_tensor("arena", [128, 16384], BF16))
        q_sb = E(nc.sbuf_tensor("q_sb", [128, KC * T], BF16))
        k_sb = E(nc.sbuf_tensor("k_sb", [128, KC * T], BF16))
        v_aug = E(nc.sbuf_tensor("v_aug", [128, NSC * 1024], BF16))
        exp_sb = [E(nc.sbuf_tensor(f"exp{i}", [128, 1024], BF16)) for i in range(5)]
        dtmp = E(nc.sbuf_tensor("dtmp", [128, 512], F32))
        out_t = E(nc.sbuf_tensor("out_t", [128, D], F32))
        rdp = E(nc.sbuf_tensor("rdp", [128, 512], F32))
        ones128 = E(nc.sbuf_tensor("ones128", [128, 64], BF16))
        ident_sb = E(nc.sbuf_tensor("ident_sb", [128, 128], BF16))
        bu_sb = E(nc.sbuf_tensor("bu_sb", [128, D], F32))
        wsc_sb = E(nc.sbuf_tensor("wsc_sb", [128, D], F32))
        bia_sb = E(nc.sbuf_tensor("bia_sb", [128, D], F32))
        bns1 = [E(nc.sbuf_tensor(f"bns1_{i}", [128, 12], F32)) for i in range(2)]
        mv1 = [E(nc.sbuf_tensor(f"mv1_{i}", [128, 2], F32)) for i in range(2)]
        rin1 = [E(nc.sbuf_tensor(f"rin1_{i}", [128, 1], F32)) for i in range(2)]
        rln1 = [E(nc.sbuf_tensor(f"rln1_{i}", [128, 1], F32)) for i in range(2)]
        rstd1 = [E(nc.sbuf_tensor(f"rstd1_{i}", [128, 1], F32)) for i in range(2)]
        bns2 = [E(nc.sbuf_tensor(f"bns2_{i}", [128, 12], F32)) for i in range(2)]
        mv2 = [E(nc.sbuf_tensor(f"mv2_{i}", [128, 2], F32)) for i in range(2)]
        rin2 = [E(nc.sbuf_tensor(f"rin2_{i}", [128, 1], F32)) for i in range(2)]
        rln2 = [E(nc.sbuf_tensor(f"rln2_{i}", [128, 1], F32)) for i in range(2)]
        rstd2 = [E(nc.sbuf_tensor(f"rstd2_{i}", [128, 1], F32)) for i in range(2)]
        amx = [E(nc.sbuf_tensor(f"amx{i}", [128, 1], F32)) for i in range(2)]
        xsv = [E(nc.sbuf_tensor(f"xsv{i}", [128, 1], F32)) for i in range(2)]
        rxs = [E(nc.sbuf_tensor(f"rxs{i}", [128, 1], F32)) for i in range(2)]

        sc_ps = [E(nc.psum_tensor(f"sc_ps{i}", [128, 1024], F32)) for i in range(2)]
        out_pair = [E(nc.psum_tensor(f"op{i}", [128, 512], F32)) for i in range(2)]
        den_ps = E(nc.psum_tensor("den_ps", [128, 512], F32))
        p3_ps = E(nc.psum_tensor("p3_ps", [128, 512], F32))

        sem = {n: E(nc.semaphore(n)) for n in Sched.SEMS}

        a = arena.ap()
        af = a.bitcast(F32)
        ai = a.bitcast(I32)
        x_fmb = a
        outn = [a[:, 0:4096], a[:, 4096:8192]]
        t1_v = af[:, 4096:5120]
        xqi_v = ai[:, 4096:5120]
        f2_v = af[:, 4096:5120]
        z1_v = af[:, 5120:6144]
        tq_v = z1_v
        ybb_v = z1_v
        outt_v = out_t.ap()
        y_v = af[:, 6144:7168]
        z2_v = y_v
        xqb_v = a[:, 14336:15360]
        xqT_v = a[:, 15360:16384]
        xqT_ps = p3_ps.ap().bitcast(BF16)

        S = Sched()

        S.emit("gpsimd", lambda g: g.memset(ones128[:], 1.0))
        setup_pool = S.t["pool"]

        for dst, src in ((bu_sb, bu_b), (wsc_sb, ws_b), (bia_sb, bias_b),
                         (ident_sb, ident)):
            t_ = S.emit("sync", lambda s, d=dst, sr=src: s.dma_start(out=d[:], in_=sr),
                        sem="set")
        setup_ld = ("set", t_)

        ring8 = [stage[0][:, 0:512], stage[0][:, 512:1024],
                 stage[1][:, 0:512], stage[1][:, 512:1024],
                 xres[0][:, 0:512], xres[0][:, 512:1024],
                 xres[1][:, 0:512], xres[1][:, 512:1024],
                 dtmp[:], rdp[:]]
        RING8_SEMS = ["ws0", "ws1", "xs0", "xs1", "xr0", "xr1", "b0", "b1",
                      "b2", "b3"]
        stage_free = [("pool", 0), ("pool", 0)]
        ring_free = [("pool", 0)] * 10
        stage_n = [0]
        borrow_free = [("pool", 0)]
        xres_free = [("pool", 0), ("pool", 0)]
        prev_t5 = [0]
        p1_free = [0, 0, 0, 0]
        exp_tick = [0, 0]
        attv_tick = [0] * 5
        op_free = [0, 0]
        den_free = [0]
        rdp_free = [0]
        dtmp_free = [0]
        bank7 = [0]
        last_store = [0]
        outn_rd = [0, 0]
        rin1_free = [0, 0]
        rstd1_free = [0, 0]
        rin2_free = [0, 0]
        rstd2_free = [0, 0]
        astep = [0]

        def load_weight(w_ap, slot, is_int8, war_pe, extra=(), dma_extra=()):
            last = 0
            nch = KC if is_int8 else 2 * KC
            for ch in range(nch):
                if is_int8:
                    si = stage_n[0] % 4
                    if si >= 2:
                        si = si % 2
                    stage_n[0] += 4 - stage_n[0] % 2 if False else 1
                    src = w_ap[128 * ch:128 * (ch + 1), :]
                    dst = stage[si].ap().bitcast(I8)[:, 0:D]
                    fslot = 2 * si
                    sm = RING8_SEMS[fslot]
                    so, w_ = D * ch, D
                    wfree = [ring_free[2 * si], ring_free[2 * si + 1]]
                else:
                    fslot = stage_n[0] % 10
                    stage_n[0] += 1
                    kc, hh = ch // 2, ch % 2
                    src = w_ap[128 * kc:128 * (kc + 1), 512 * hh:512 * (hh + 1)]
                    dst = ring8[fslot]
                    sm = RING8_SEMS[fslot]
                    so, w_ = D * kc + 512 * hh, 512
                    wfree = [ring_free[fslot]]
                tl = S.emit("sync", lambda s, d=dst, sr=src: s.dma_start(out=d, in_=sr),
                            waits=wfree + list(dma_extra), sem=sm)
                waits = [(sm, tl)]
                if ch == 0:
                    if war_pe:
                        waits.append(("pe", war_pe))
                    waits.extend(extra)
                tp = S.emit("vector", lambda g, sl=slot, o=so, ww=w_, s_=dst:
                            g.tensor_copy(out=wslot[sl][:, o:o + ww], in_=s_),
                            waits=waits, nosync=True)
                tk = ("dve", tp)
                if is_int8:
                    ring_free[2 * si] = tk
                    ring_free[2 * si + 1] = tk
                else:
                    ring_free[fslot] = tk
                    if fslot >= 4:
                        borrow_free[0] = tk
                last = tp
            return [("dve", last)]

        def load_x_half(seq, hh, extra=(), dma_extra=()):
            last = 0
            for pc in range(KC):
                for qq in range(2):
                    fslot = stage_n[0] % 10
                    stage_n[0] += 1
                    c0 = 1024 * hh + 512 * qq
                    src = x_fm[seq, 128 * pc:128 * (pc + 1), c0:c0 + 512]
                    d_ap = ring8[fslot]
                    sm = RING8_SEMS[fslot]
                    tl = S.emit("sync", lambda s, d=d_ap, sr=src:
                                s.dma_start(out=d, in_=sr),
                                waits=[ring_free[fslot]] + list(dma_extra), sem=sm)
                    w = ([(sm, tl)]
                         + (list(extra) if pc == 0 and qq == 0 else []))
                    tp = S.emit("gpsimd", lambda g, o=2048 * pc + c0, s_=d_ap:
                                g.tensor_copy(out=x_fmb[:, o:o + 512], in_=s_),
                                waits=w, nosync=True)
                    ring_free[fslot] = ("pool", tp)
                    if fslot >= 4:
                        borrow_free[0] = ("pool", tp)
                    last = tp
            return [("pool", last)]

        for seq in range(NSEQ):
            bar = [] if seq == 0 else [("pe", S.t["pe"]), ("dve", S.t["dve"]),
                                       ("act", S.t["act"]), ("st", S.t["st"])]

            dx = bar if seq == 1 else ()
            wq_ready = load_weight(wq, 0, False, None, extra=bar, dma_extra=dx)
            wk_ready = load_weight(wk, 1, False, None, dma_extra=dx)
            xf_ready = [load_x_half(seq, 0, dma_extra=dx),
                        load_x_half(seq, 1, dma_extra=dx)]

            def proj_fm(w_slot, w_ready, dest_sb, scale):
                last_pe = 0
                gi = [0]
                for tg in range(NTG):
                    for oc in range(KC):
                        slot = gi[0] % 4
                        gi[0] += 1
                        ps = sc_ps[slot // 2][:, 512 * (slot % 2):512 * (slot % 2) + 512]
                        for kc in range(KC):
                            waits = []
                            if kc == 0:
                                waits = list(w_ready) + list(xf_ready[tg // 2]) + [
                                    ("dve", p1_free[slot])]
                            tpe = S.emit("tensor", lambda te, p=ps,
                                         wo=D * kc + 128 * oc, xo=2048 * kc + 512 * tg,
                                         sl=w_slot, st_=(kc == 0), sp=(kc == KC - 1):
                                         te.matmul(p, wslot[sl][:, wo:wo + 128],
                                                   x_fmb[:, xo:xo + 512],
                                                   start=st_, stop=sp),
                                         waits=waits)
                        if scale is None:
                            td = S.emit("vector", lambda v, p=ps, d=dest_sb,
                                        o=2048 * oc + 512 * tg:
                                        v.tensor_copy(out=d[:, o:o + 512], in_=p),
                                        waits=[("pe", tpe)], nosync=True)
                        else:
                            td = S.emit("vector", lambda v, p=ps, d=dest_sb,
                                        o=2048 * oc + 512 * tg, sc=scale:
                                        v.tensor_scalar_mul(out=d[:, o:o + 512], in0=p,
                                                            scalar1=sc),
                                        waits=[("pe", tpe)], nosync=True)
                        p1_free[slot] = td
                        last_pe = tpe
                return last_pe

            q_last_pe = proj_fm(0, wq_ready, q_sb, 2.0 ** -5)
            wv_ready = load_weight(wv, 0, False, q_last_pe, dma_extra=dx)
            k_last_pe = proj_fm(1, wk_ready, k_sb, None)
            wu_ready = load_weight(wu, 1, False, k_last_pe, dma_extra=dx)

            v_last_pe = 0
            gi = 0
            for tc in range(NTC):
                for og in range(2):
                    slot = gi % 4
                    gi += 1
                    ps = sc_ps[slot // 2][:, 512 * (slot % 2):512 * (slot % 2) + 512]
                    for kc in range(KC):
                        waits = []
                        if kc == 0:
                            waits = (list(wv_ready) + list(xf_ready[1]) +
                                     [("pool", setup_pool), ("dve", p1_free[slot])])
                        tpe = S.emit("tensor", lambda te, p=ps, xo=2048 * kc + 128 * tc,
                                     wo=D * kc + 512 * og, st_=(kc == 0), sp=(kc == KC - 1):
                                     te.matmul(p, x_fmb[:, xo:xo + 128],
                                               wslot[0][:, wo:wo + 512],
                                               start=st_, stop=sp),
                                     waits=waits)
                    v_last_pe = tpe
                    td = S.emit("vector", lambda v, p=ps, do=1024 * tc + 512 * og:
                                v.tensor_copy(out=v_aug[:, do:do + 512], in_=p),
                                waits=[("pe", tpe)], nosync=True)
                    p1_free[slot] = td
            w8_ready = load_weight(w8, 0, True, v_last_pe, dma_extra=dx)

            q_ready = S.t["dve"]
            first_att = [True]

            def p3_micros(tg_, tc):
                par = tg_ % 2
                r = tc % 2
                st = {}

                def u_half(og, half, extra_waits):
                    tpe = 0
                    for kc in range(4 * half, 4 * half + 4):
                        waits = list(extra_waits) if kc == 4 * half else []
                        tpe = S.emit("tensor", lambda te, oo=512 * kc + 128 * (tc % 4),
                                     wo=D * kc + 512 * og, p=par,
                                     st_=(kc == 0), sp=(kc == KC - 1):
                                     te.matmul(p3_ps[:], outn[p][:, oo:oo + 128],
                                               wslot[1][:, wo:wo + 512],
                                               start=st_, stop=sp),
                                     waits=waits)
                    return tpe

                def m1():
                    st["tl"] = S.emit("sync", lambda s, sq=seq: s.dma_start(
                        out=xres[r][:], in_=x_tm[sq, 128 * tc:128 * (tc + 1), :]),
                        waits=[xres_free[r], borrow_free[0]],
                        sem=f"xr{r}")
                    u_half(0, 0, [("dve", bank7[0]), ("dve", outn_done[par])]
                           + list(wu_ready))

                def m1b():
                    tpe = u_half(0, 1, [])
                    outn_rd[par] = tpe
                    st["ev0"] = S.emit("vector", lambda v:
                                       v.tensor_tensor(out=t1_v[:, 0:512], in0=p3_ps[:],
                                                       in1=bu_sb[:, 0:512], op=ALU.add),
                                       waits=[("pe", tpe), setup_ld], nosync=True)

                def m2():
                    u_half(1, 0, [("dve", st["ev0"])])

                def m2b():
                    tpe = u_half(1, 1, [])
                    outn_rd[par] = tpe
                    st["ev1"] = S.emit("vector", lambda v:
                                       v.tensor_tensor(out=t1_v[:, 512:1024], in0=p3_ps[:],
                                                       in1=bu_sb[:, 512:1024], op=ALU.add),
                                       waits=[("pe", tpe)], nosync=True)
                    tz = S.emit("gpsimd", lambda v: v.tensor_tensor(
                        out=z1_v[:], in0=t1_v[:], in1=xres[r][:], op=ALU.add),
                        waits=[(f"xr{r}", st["tl"]), ("dve", st["ev1"])])
                    st["tz"] = tz
                    xres_free[r] = ("pool", tz)

                def m3():
                    S.emit("vector", lambda v: v.bn_stats(out=bns1[r][:, 0:6],
                                                          in_=z1_v[:, 0:512]),
                           waits=[("pool", st["tz"])])
                    S.emit("vector", lambda v: v.bn_stats(out=bns1[r][:, 6:12],
                                                          in_=z1_v[:, 512:1024]))
                    S.emit("vector", lambda v: v.bn_aggr(out=mv1[r][:], in_=bns1[r][:]))
                    st["tv1"] = S.emit("vector", lambda v: v.tensor_scalar_add(
                        out=rin1[r][:], in0=mv1[r][:, 1:2], scalar1=EPS_LN),
                        waits=[("act", rin1_free[r])])

                def m3b():
                    rin1_free[r] = S.emit("scalar", lambda sc: sc.activation(
                        out=rln1[r][:], in_=rin1[r][:], func=AF.Ln),
                        waits=[("dve", st["tv1"])], nosync=True)
                    tex = S.emit("scalar", lambda sc: sc.activation(
                        out=rstd1[r][:], in_=rln1[r][:], func=AF.Exp, scale=-0.5),
                        waits=[("dve", rstd1_free[r])])
                    ty = S.emit("vector", lambda v: v.tensor_scalar(
                        out=y_v[:], in0=z1_v[:], scalar1=mv1[r][:, 0:1],
                        scalar2=rstd1[r][:],
                        op0=ALU.subtract, op1=ALU.mult), waits=[("act", tex)])
                    rstd1_free[r] = ty
                    st["ty"] = ty
                    S.emit("vector", lambda v: v.reduce_max(
                        out=amx[r][:], in_=y_v[:], axis=AX.X, apply_absolute_value=True))
                    S.emit("vector", lambda v: v.tensor_scalar(
                        out=xsv[r][:], in0=amx[r][:], scalar1=EPS_Q, scalar2=1.0 / 127.0,
                        op0=ALU.max, op1=ALU.mult))
                    S.emit("vector", lambda v: v.reciprocal(out=rxs[r][:], in_=xsv[r][:]))

                def m4():
                    S.emit("vector", lambda v: v.tensor_scalar_mul(
                        out=tq_v[:], in0=y_v[:], scalar1=rxs[r][:]))
                    ti = S.emit("vector", lambda v: v.tensor_copy(
                        out=xqi_v, in_=tq_v[:]))
                    st["xq"] = S.emit("gpsimd", lambda v: v.tensor_copy(
                        out=xqb_v, in_=xqi_v),
                        waits=[("dve", ti), ("pe", prev_t5[0])])
                    st["ybb"] = S.emit("gpsimd", lambda v: v.tensor_tensor(
                        out=ybb_v[:], in0=y_v[:], in1=bia_sb[:], op=ALU.add),
                        waits=[("dve", ti), ("dve", st["ty"]), setup_ld])

                def m5():
                    for jj in range(4):
                        waits = []
                        if jj == 0:
                            waits = [("dve", st["ev1"]), ("pool", st["xq"]),
                                     setup_ld]
                        S.emit("tensor", lambda te, o=128 * jj:
                               te.transpose(xqT_ps[:, o:o + 128],
                                            xqb_v[:, o:o + 128], ident_sb[:]),
                               waits=waits)

                def m5b():
                    tpe = 0
                    for jj in range(4, KC):
                        tpe = S.emit("tensor", lambda te, o=128 * jj:
                                     te.transpose(xqT_ps[:, o:o + 128],
                                                  xqb_v[:, o:o + 128], ident_sb[:]))
                    prev_t5[0] = tpe
                    st["xqT"] = S.emit("vector", lambda v: v.tensor_copy(
                        out=xqT_v, in_=xqT_ps[:]), waits=[("pe", tpe)], nosync=True)

                def f_half(og, half, extra_waits):
                    tpe = 0
                    for kc in range(4 * half, 4 * half + 4):
                        waits = list(extra_waits) if kc == 4 * half else []
                        tpe = S.emit("tensor", lambda te, xo=128 * kc,
                                     wo=D * kc + 512 * og,
                                     st_=(kc == 0), sp=(kc == KC - 1):
                                     te.matmul(p3_ps[:], xqT_v[:, xo:xo + 128],
                                               wslot[0][:, wo:wo + 512],
                                               start=st_, stop=sp),
                                     waits=waits)
                    return tpe

                def m6():
                    f_half(0, 0, [("dve", st["xqT"])] + list(w8_ready))

                def m6b():
                    tpe = f_half(0, 1, [])
                    st["dqA"] = S.emit("vector", lambda v: v.scalar_tensor_tensor(
                        out=f2_v[:, 0:512], in0=p3_ps[:], scalar=xsv[r][:],
                        in1=wsc_sb[:, 0:512], op0=ALU.mult, op1=ALU.mult),
                        waits=[("pe", tpe)], nosync=True)

                def m7():
                    f_half(1, 0, [("dve", st["dqA"])])

                def m7a():
                    tpe = f_half(1, 1, [])
                    dq = S.emit("vector", lambda v: v.scalar_tensor_tensor(
                        out=f2_v[:, 512:1024], in0=p3_ps[:], scalar=xsv[r][:],
                        in1=wsc_sb[:, 512:1024], op0=ALU.mult, op1=ALU.mult),
                        waits=[("pe", tpe)], nosync=True)
                    bank7[0] = dq
                    tz2 = S.emit("gpsimd", lambda v: v.tensor_tensor(
                        out=z2_v[:], in0=f2_v[:], in1=ybb_v[:], op=ALU.add),
                        waits=[("dve", dq)])
                    S.emit("vector", lambda v: v.bn_stats(out=bns2[r][:, 0:6],
                                                          in_=z2_v[:, 0:512]),
                           waits=[("pool", tz2)])
                    S.emit("vector", lambda v: v.bn_stats(out=bns2[r][:, 6:12],
                                                          in_=z2_v[:, 512:1024]))
                    S.emit("vector", lambda v: v.bn_aggr(out=mv2[r][:], in_=bns2[r][:]))
                    st["tv2"] = S.emit("vector", lambda v: v.tensor_scalar_add(
                        out=rin2[r][:], in0=mv2[r][:, 1:2], scalar1=EPS_LN),
                        waits=[("act", rin2_free[r])])

                def m7b():
                    rin2_free[r] = S.emit("scalar", lambda sc: sc.activation(
                        out=rln2[r][:], in_=rin2[r][:], func=AF.Ln),
                        waits=[("dve", st["tv2"])], nosync=True)
                    tex = S.emit("scalar", lambda sc: sc.activation(
                        out=rstd2[r][:], in_=rln2[r][:], func=AF.Exp, scale=-0.5),
                        waits=[("dve", rstd2_free[r])])
                    tno = S.emit("vector", lambda v: v.tensor_scalar(
                        out=outt_v[:], in0=z2_v[:], scalar1=mv2[r][:, 0:1],
                        scalar2=rstd2[r][:],
                        op0=ALU.subtract, op1=ALU.mult),
                        waits=[("act", tex), ("st", last_store[0])])
                    rstd2_free[r] = tno
                    ts = S.emit("sync", lambda s, sq=seq: s.dma_start(
                        out=out[sq, 128 * tc:128 * (tc + 1), :], in_=outt_v[:]),
                        waits=[("dve", tno)], sem="st")
                    last_store[0] = ts

                gap = lambda: None
                return [m1, m1b, m2, m2b, m3, gap, gap, m3b, m4, m5, m5b, m6,
                        m6b, m7, m7a, gap, gap, m7b]

            outn_done = [0, 0]
            p3_queue = []

            def emit_attv(j, sc, tg, buf, te_tick):
                hA, hB = 2 * j, 2 * j + 1
                par = j % 2
                op = out_pair[par]
                wA = [("act", te_tick)]
                if sc == 0:
                    wA.append(("dve", op_free[par]))
                S.emit("tensor", lambda te, vo=1024 * sc + 64 * hA, b=buf,
                       st_=(sc == 0), sp=(sc == NSC - 1):
                       te.matmul(op[0:64, :], v_aug[:, vo:vo + 64],
                                 exp_sb[b][:, 0:512], start=st_, stop=sp,
                                 tile_position=(0, 0), skip_group_check=True),
                       waits=wA)
                S.emit("tensor", lambda te, vo=1024 * sc + 64 * hB, b=buf,
                       st_=(sc == 0), sp=(sc == NSC - 1):
                       te.matmul(op[64:128, :], v_aug[:, vo:vo + 64],
                                 exp_sb[b][:, 512:1024], start=st_, stop=sp,
                                 tile_position=(0, 64), skip_group_check=True))
                wD = [("act", den_free[0])] if sc == 0 else []
                S.emit("tensor", lambda te, b=buf, st_=(sc == 0), sp=(sc == NSC - 1):
                       te.matmul(den_ps[0:64, :], ones128[:, 0:64],
                                 exp_sb[b][:, 0:512], start=st_, stop=sp,
                                 tile_position=(0, 0), skip_group_check=True),
                       waits=wD)
                tdn = S.emit("tensor", lambda te, b=buf, st_=(sc == 0), sp=(sc == NSC - 1):
                             te.matmul(den_ps[64:128, :], ones128[:, 0:64],
                                       exp_sb[b][:, 512:1024], start=st_, stop=sp,
                                       tile_position=(0, 64), skip_group_check=True))
                attv_tick[buf] = tdn
                return tdn

            def emit_epi_a(tdn):
                tln = S.emit("scalar", lambda s_: s_.activation(
                    out=dtmp[:], in_=den_ps[:], func=AF.Ln),
                    waits=[("pe", tdn), ("act", dtmp_free[0]), borrow_free[0]],
                    nosync=True)
                den_free[0] = tln
                return tln

            def emit_epi_b(j, tg, tln):
                hA, hB = 2 * j, 2 * j + 1
                par = tg % 2
                opp = out_pair[j % 2]
                tre = S.emit("scalar", lambda s_: s_.activation(
                    out=rdp[:], in_=dtmp[:], func=AF.Exp, scale=-1.0),
                    waits=[("dve", rdp_free[0]), borrow_free[0]])
                dtmp_free[0] = tre
                ttA = S.emit("vector", lambda v, h=hA, p=par: v.tensor_tensor(
                    out=outn[p][(h % 2) * 64:(h % 2) * 64 + 64,
                                512 * (h // 2):512 * (h // 2) + 512],
                    in0=opp[0:64, :], in1=rdp[0:64, :], op=ALU.mult),
                    waits=[("pe", outn_rd[par]), ("act", tre)], nosync=True)
                ttB = S.emit("vector", lambda v, h=hB, p=par: v.tensor_tensor(
                    out=outn[p][(h % 2) * 64:(h % 2) * 64 + 64,
                                512 * (h // 2):512 * (h // 2) + 512],
                    in0=opp[64:128, :], in1=rdp[64:128, :], op=ALU.mult), nosync=True)
                op_free[j % 2] = ttB
                rdp_free[0] = ttB
                if j == 7:
                    outn_done[par] = ttB

            pending = []
            pending_b = [None]

            for tg in range(NTG):
                for j in range(8):
                    for sc in range(NSC):
                        sbuf_i = astep[0] % 2
                        ebuf = astep[0] % 5
                        astep[0] += 1
                        wsc = [("act", exp_tick[sbuf_i])]
                        if first_att[0]:
                            wsc.append(("dve", q_ready))
                            first_att[0] = False
                        S.emit("tensor", lambda te, ko=2048 * j + 128 * sc,
                               qo=2048 * j + 512 * tg, b=sbuf_i:
                               te.matmul(sc_ps[b][:, 0:512], k_sb[0:64, ko:ko + 128],
                                         q_sb[0:64, qo:qo + 512], start=True, stop=True),
                               waits=wsc)
                        t2m = S.emit("tensor", lambda te, ko=2048 * j + 128 * sc,
                                     qo=2048 * j + 512 * tg, b=sbuf_i:
                                     te.matmul(sc_ps[b][:, 512:1024],
                                               k_sb[64:128, ko:ko + 128],
                                               q_sb[64:128, qo:qo + 512],
                                               start=True, stop=True))
                        te_ = S.emit("scalar", lambda s_, b=sbuf_i, e=ebuf: s_.activation(
                            out=exp_sb[e][:], in_=sc_ps[b][:], func=AF.Exp),
                            waits=[("pe", t2m), ("pe", attv_tick[ebuf])], nosync=True)
                        exp_tick[sbuf_i] = te_
                        if sc == 2 and pending_b[0] is not None:
                            pending_b[0]()
                            pending_b[0] = None
                        if sc % 2 == 1 and (j > 0 or sc >= 3) and p3_queue:
                            p3_queue.pop(0)()
                        if len(pending) >= 2:
                            pending.pop(0)()
                        def deferred(j_=j, sc_=sc, tg_=tg, e_=ebuf, tk=te_):
                            tdn = emit_attv(j_, sc_, tg_, e_, tk)
                            if sc_ == NSC - 1:
                                tre = emit_epi_a(tdn)
                                pending_b[0] = (lambda jj=j_, tt=tg_, tr=tre:
                                                emit_epi_b(jj, tt, tr))
                        pending.append(deferred)

                for i in range(4):
                    p3_queue.extend(p3_micros(tg, 4 * tg + i))

            while pending:
                pending.pop(0)()
            if pending_b[0] is not None:
                pending_b[0]()
                pending_b[0] = None
            while p3_queue:
                p3_queue.pop(0)()

        with nc.Block() as block:
            def replay(eng_name, eng):
                own = ENG_SEM[eng_name]
                last_wait = {}
                self_serial = eng_name in ("vector", "scalar", "gpsimd")
                k = 0
                for builder, waits, sname, inc, nosync in S.ops[eng_name]:
                    if self_serial and k > 0 and not nosync:
                        eng.wait_ge(sem[own], k)
                    k += inc
                    for w_sem, val in waits:
                        if w_sem == own:
                            continue
                        if last_wait.get(w_sem, -1) >= val:
                            continue
                        eng.wait_ge(sem[w_sem], val)
                        last_wait[w_sem] = val
                    builder(eng).then_inc(sem[sname], inc)

            total_st = S.t["st"]

            @block.sync
            def _(se):
                replay("sync", se)
                se.wait_ge(sem["st"], total_st)

            @block.tensor
            def _(te):
                replay("tensor", te)

            @block.vector
            def _(v):
                replay("vector", v)

            @block.scalar
            def _(sc):
                replay("scalar", sc)

            @block.gpsimd
            def _(g):
                replay("gpsimd", g)

    return nc


_NC_CACHE = {}


def _get_nc():
    if "nc" not in _NC_CACHE:
        _NC_CACHE["nc"] = build_program()
    return _NC_CACHE["nc"]


def kernel(L0_attn_x, L0_attn_Wq, L0_attn_Wk, L0_attn_Wv, L0_attn_Wu, L0_attn_bu,
           L0_ffn_weight_q, L0_ffn_weight_scale, L0_ffn_bias):
    x = np.asarray(L0_attn_x, np.float32)
    wqT = np.ascontiguousarray(np.asarray(L0_attn_Wq, np.float32).T)
    wkT = np.ascontiguousarray(np.asarray(L0_attn_Wk, np.float32).T)
    wvT = np.ascontiguousarray(np.asarray(L0_attn_Wv, np.float32).T)
    wuT = np.ascontiguousarray(np.asarray(L0_attn_Wu, np.float32).T)
    w8T = np.ascontiguousarray(np.asarray(L0_ffn_weight_q, np.int8).T)
    bu_b = np.ascontiguousarray(np.broadcast_to(
        np.asarray(L0_attn_bu, np.float32)[None, :], (128, D)))
    ws_b = np.ascontiguousarray(np.broadcast_to(
        np.asarray(L0_ffn_weight_scale, np.float32)[None, :], (128, D)))
    bias_b = np.ascontiguousarray(np.broadcast_to(
        np.asarray(L0_ffn_bias, np.float32)[None, :], (128, D)))
    ident = np.eye(128).astype(bf)

    nc = _get_nc()
    in_maps = []
    for c in range(8):
        xc = x[2 * c:2 * c + 2]
        in_maps.append({
            "x_fm": np.ascontiguousarray(xc.transpose(0, 2, 1)),
            "x_tm": np.ascontiguousarray(xc),
            "wq": wqT, "wk": wkT, "wv": wvT, "wu": wuT, "w8": w8T,
            "bu_b": bu_b, "ws_b": ws_b, "bias_b": bias_b, "ident": ident,
        })
    res = run_bass_kernel_spmd(nc, in_maps, core_ids=list(range(8)))
    return np.concatenate([np.asarray(r["out"], np.float32) for r in res.results],
                          axis=0)


if __name__ == "__main__":
    print("building program...")
    nc = build_program()
    print("instructions:", nc.n_instructions())


# version 22
# speedup vs baseline: 1.0784x; 1.0784x over previous
import numpy as np
import ml_dtypes
import concourse.bass as bass
import concourse.mybir as mybir
from concourse.bass_utils import run_bass_kernel_spmd
from contextlib import ExitStack

F32 = mybir.dt.float32
BF16 = mybir.dt.bfloat16
I8 = mybir.dt.int8
I32 = mybir.dt.int32
AF = mybir.ActivationFunctionType
ALU = mybir.AluOpType
AX = mybir.AxisListType
bf = ml_dtypes.bfloat16

D = 1024
T = 2048
NSEQ = 2
H = 16
HD = 64
KC = 8
NTG = 4
NTC = 16
NSC = 16
EPS_LN = 1e-5
EPS_Q = 1e-6
VW = 1040

ENG_SEM = {"sync": None, "tensor": "pe", "vector": "dve", "scalar": "act", "gpsimd": "pool"}


class Sched:

    SEMS = ("set", "ws0", "ws1", "xs0", "xs1", "xr0", "xr1", "b0", "b1",
            "b2", "b3", "st", "pe", "act", "dve", "pool")

    def __init__(self):
        self.ops = {e: [] for e in ("sync", "tensor", "vector", "scalar", "gpsimd")}
        self.t = {s: 0 for s in self.SEMS}

    def emit(self, eng, builder, waits=(), sem=None, inc=None, nosync=False):
        if sem is None:
            sem = ENG_SEM[eng]
        if inc is None:
            inc = 1 if sem in ("pe", "act", "dve", "pool") else 16
        w = [(s, v) for (s, v) in waits if v and v > 0]
        self.ops[eng].append((builder, w, sem, inc, nosync))
        self.t[sem] += inc
        return self.t[sem]


def build_program():
    nc = bass.Bass("TRN2", target_bir_lowering=False, debug=False)

    x_fm = nc.dram_tensor("x_fm", [NSEQ, D, T], F32, kind="ExternalInput").ap()
    x_tm = nc.dram_tensor("x_tm", [NSEQ, T, D], F32, kind="ExternalInput").ap()
    wq = nc.dram_tensor("wq", [D, D], F32, kind="ExternalInput").ap()
    wk = nc.dram_tensor("wk", [D, D], F32, kind="ExternalInput").ap()
    wv = nc.dram_tensor("wv", [D, D], F32, kind="ExternalInput").ap()
    wu = nc.dram_tensor("wu", [D, D], F32, kind="ExternalInput").ap()
    w8 = nc.dram_tensor("w8", [D, D], I8, kind="ExternalInput").ap()
    bu_b = nc.dram_tensor("bu_b", [128, D], F32, kind="ExternalInput").ap()
    ws_b = nc.dram_tensor("ws_b", [128, D], F32, kind="ExternalInput").ap()
    bias_b = nc.dram_tensor("bias_b", [128, D], F32, kind="ExternalInput").ap()
    ident = nc.dram_tensor("ident", [128, 128], BF16, kind="ExternalInput").ap()
    out = nc.dram_tensor("out", [NSEQ, T, D], F32, kind="ExternalOutput").ap()

    with ExitStack() as ctx:
        E = ctx.enter_context

        wslot = [E(nc.sbuf_tensor(f"wslot{i}", [128, KC * D], BF16)) for i in range(2)]
        stage = [E(nc.sbuf_tensor(f"stage{i}", [128, 1024], F32)) for i in range(2)]
        xres = [E(nc.sbuf_tensor(f"xres{i}", [128, D], F32)) for i in range(2)]
        arena = E(nc.sbuf_tensor("arena", [128, 16384], BF16))
        q_sb = E(nc.sbuf_tensor("q_sb", [128, KC * T], BF16))
        k_sb = E(nc.sbuf_tensor("k_sb", [128, KC * T], BF16))
        v_aug = E(nc.sbuf_tensor("v_aug", [128, NSC * 1024], BF16))
        exp_sb = [E(nc.sbuf_tensor(f"exp{i}", [128, 1024], BF16)) for i in range(5)]
        dtmp = E(nc.sbuf_tensor("dtmp", [128, 512], F32))
        out_t = E(nc.sbuf_tensor("out_t", [128, D], F32))
        rdp = E(nc.sbuf_tensor("rdp", [128, 512], F32))
        ones128 = E(nc.sbuf_tensor("ones128", [128, 64], BF16))
        ident_sb = E(nc.sbuf_tensor("ident_sb", [128, 128], BF16))
        bu_sb = E(nc.sbuf_tensor("bu_sb", [128, D], F32))
        wsc_sb = E(nc.sbuf_tensor("wsc_sb", [128, D], F32))
        bia_sb = E(nc.sbuf_tensor("bia_sb", [128, D], F32))
        bns1 = [E(nc.sbuf_tensor(f"bns1_{i}", [128, 12], F32)) for i in range(2)]
        mv1 = [E(nc.sbuf_tensor(f"mv1_{i}", [128, 2], F32)) for i in range(2)]
        rin1 = [E(nc.sbuf_tensor(f"rin1_{i}", [128, 1], F32)) for i in range(2)]
        rln1 = [E(nc.sbuf_tensor(f"rln1_{i}", [128, 1], F32)) for i in range(2)]
        rstd1 = [E(nc.sbuf_tensor(f"rstd1_{i}", [128, 1], F32)) for i in range(2)]
        bns2 = [E(nc.sbuf_tensor(f"bns2_{i}", [128, 12], F32)) for i in range(2)]
        mv2 = [E(nc.sbuf_tensor(f"mv2_{i}", [128, 2], F32)) for i in range(2)]
        rin2 = [E(nc.sbuf_tensor(f"rin2_{i}", [128, 1], F32)) for i in range(2)]
        rln2 = [E(nc.sbuf_tensor(f"rln2_{i}", [128, 1], F32)) for i in range(2)]
        rstd2 = [E(nc.sbuf_tensor(f"rstd2_{i}", [128, 1], F32)) for i in range(2)]
        amx = [E(nc.sbuf_tensor(f"amx{i}", [128, 1], F32)) for i in range(2)]
        xsv = [E(nc.sbuf_tensor(f"xsv{i}", [128, 1], F32)) for i in range(2)]
        rxs = [E(nc.sbuf_tensor(f"rxs{i}", [128, 1], F32)) for i in range(2)]

        sc_ps = [E(nc.psum_tensor(f"sc_ps{i}", [128, 1024], F32)) for i in range(2)]
        out_pair = [E(nc.psum_tensor(f"op{i}", [128, 512], F32)) for i in range(2)]
        den_ps = E(nc.psum_tensor("den_ps", [128, 512], F32))
        p3_ps = E(nc.psum_tensor("p3_ps", [128, 512], F32))

        sem = {n: E(nc.semaphore(n)) for n in Sched.SEMS}

        a = arena.ap()
        af = a.bitcast(F32)
        ai = a.bitcast(I32)
        x_fmb = a
        outn = [a[:, 0:4096], a[:, 4096:8192]]
        t1_v = af[:, 4096:5120]
        xqi_v = ai[:, 4096:5120]
        f2_v = af[:, 4096:5120]
        z1_v = af[:, 5120:6144]
        tq_v = z1_v
        ybb_v = z1_v
        outt_v = out_t.ap()
        y_v = af[:, 6144:7168]
        z2_v = y_v
        xqb_v = a[:, 14336:15360]
        xqT_v = a[:, 15360:16384]
        xqT_ps = p3_ps.ap().bitcast(BF16)

        S = Sched()

        S.emit("gpsimd", lambda g: g.memset(ones128[:], 1.0))
        setup_pool = S.t["pool"]

        for dst, src in ((bu_sb, bu_b), (wsc_sb, ws_b), (bia_sb, bias_b),
                         (ident_sb, ident)):
            t_ = S.emit("sync", lambda s, d=dst, sr=src: s.dma_start(out=d[:], in_=sr),
                        sem="set")
        setup_ld = ("set", t_)

        ring8 = [stage[0][:, 0:512], stage[0][:, 512:1024],
                 stage[1][:, 0:512], stage[1][:, 512:1024],
                 xres[0][:, 0:512], xres[0][:, 512:1024],
                 xres[1][:, 0:512], xres[1][:, 512:1024],
                 dtmp[:], rdp[:]]
        RING8_SEMS = ["ws0", "ws1", "xs0", "xs1", "xr0", "xr1", "b0", "b1",
                      "b2", "b3"]
        stage_free = [("pool", 0), ("pool", 0)]
        ring_free = [("pool", 0)] * 10
        stage_n = [0]
        borrow_free = [("pool", 0)]
        xres_free = [0, 0]
        xres_tl = {}
        p1_free = [0, 0, 0, 0]
        exp_tick = [0, 0]
        attv_tick = [0] * 5
        op_free = [0, 0]
        den_free = [0]
        rdp_free = [0]
        dtmp_free = [0]
        bank7 = [0]
        last_store = [0]
        outn_rd = [0, 0]
        rin1_free = [0, 0]
        rstd1_free = [0, 0]
        rin2_free = [0, 0]
        rstd2_free = [0, 0]
        astep = [0]

        def load_weight(w_ap, slot, is_int8, war_pe, extra=(), dma_extra=()):
            last = 0
            nch = KC if is_int8 else 2 * KC
            for ch in range(nch):
                if is_int8:
                    si = stage_n[0] % 4
                    if si >= 2:
                        si = si % 2
                    stage_n[0] += 4 - stage_n[0] % 2 if False else 1
                    src = w_ap[128 * ch:128 * (ch + 1), :]
                    dst = stage[si].ap().bitcast(I8)[:, 0:D]
                    fslot = 2 * si
                    sm = RING8_SEMS[fslot]
                    so, w_ = D * ch, D
                    wfree = [ring_free[2 * si], ring_free[2 * si + 1]]
                else:
                    fslot = stage_n[0] % 10
                    stage_n[0] += 1
                    kc, hh = ch // 2, ch % 2
                    src = w_ap[128 * kc:128 * (kc + 1), 512 * hh:512 * (hh + 1)]
                    dst = ring8[fslot]
                    sm = RING8_SEMS[fslot]
                    so, w_ = D * kc + 512 * hh, 512
                    wfree = [ring_free[fslot]]
                tl = S.emit("sync", lambda s, d=dst, sr=src: s.dma_start(out=d, in_=sr),
                            waits=wfree + list(dma_extra), sem=sm)
                waits = [(sm, tl)]
                if ch == 0:
                    if war_pe:
                        waits.append(("pe", war_pe))
                    waits.extend(extra)
                tp = S.emit("vector", lambda g, sl=slot, o=so, ww=w_, s_=dst:
                            g.tensor_copy(out=wslot[sl][:, o:o + ww], in_=s_),
                            waits=waits, nosync=True)
                tk = ("dve", tp)
                if is_int8:
                    ring_free[2 * si] = tk
                    ring_free[2 * si + 1] = tk
                else:
                    ring_free[fslot] = tk
                    if fslot >= 4:
                        borrow_free[0] = tk
                last = tp
            return [("dve", last)]

        def load_x_half(seq, hh, extra=(), dma_extra=()):
            last = 0
            for pc in range(KC):
                for qq in range(2):
                    fslot = stage_n[0] % 10
                    stage_n[0] += 1
                    c0 = 1024 * hh + 512 * qq
                    src = x_fm[seq, 128 * pc:128 * (pc + 1), c0:c0 + 512]
                    d_ap = ring8[fslot]
                    sm = RING8_SEMS[fslot]
                    tl = S.emit("sync", lambda s, d=d_ap, sr=src:
                                s.dma_start(out=d, in_=sr),
                                waits=[ring_free[fslot]] + list(dma_extra), sem=sm)
                    w = ([(sm, tl)]
                         + (list(extra) if pc == 0 and qq == 0 else []))
                    tp = S.emit("gpsimd", lambda g, o=2048 * pc + c0, s_=d_ap:
                                g.tensor_copy(out=x_fmb[:, o:o + 512], in_=s_),
                                waits=w, nosync=True)
                    ring_free[fslot] = ("pool", tp)
                    if fslot >= 4:
                        borrow_free[0] = ("pool", tp)
                    last = tp
            return [("pool", last)]

        for seq in range(NSEQ):
            bar = [] if seq == 0 else [("pe", S.t["pe"]), ("dve", S.t["dve"]),
                                       ("act", S.t["act"]), ("st", S.t["st"])]

            dx = bar if seq == 1 else ()
            wq_ready = load_weight(wq, 0, False, None, extra=bar, dma_extra=dx)
            wk_ready = load_weight(wk, 1, False, None, dma_extra=dx)
            xf_ready = [load_x_half(seq, 0, dma_extra=dx),
                        load_x_half(seq, 1, dma_extra=dx)]

            def proj_fm(w_slot, w_ready, dest_sb, scale):
                last_pe = 0
                gi = [0]
                for tg in range(NTG):
                    for oc in range(KC):
                        slot = gi[0] % 4
                        gi[0] += 1
                        ps = sc_ps[slot // 2][:, 512 * (slot % 2):512 * (slot % 2) + 512]
                        for kc in range(KC):
                            waits = []
                            if kc == 0:
                                waits = list(w_ready) + list(xf_ready[tg // 2]) + [
                                    ("dve", p1_free[slot])]
                            tpe = S.emit("tensor", lambda te, p=ps,
                                         wo=D * kc + 128 * oc, xo=2048 * kc + 512 * tg,
                                         sl=w_slot, st_=(kc == 0), sp=(kc == KC - 1):
                                         te.matmul(p, wslot[sl][:, wo:wo + 128],
                                                   x_fmb[:, xo:xo + 512],
                                                   start=st_, stop=sp),
                                         waits=waits)
                        if scale is None:
                            td = S.emit("vector", lambda v, p=ps, d=dest_sb,
                                        o=2048 * oc + 512 * tg:
                                        v.tensor_copy(out=d[:, o:o + 512], in_=p),
                                        waits=[("pe", tpe)], nosync=True)
                        else:
                            td = S.emit("vector", lambda v, p=ps, d=dest_sb,
                                        o=2048 * oc + 512 * tg, sc=scale:
                                        v.tensor_scalar_mul(out=d[:, o:o + 512], in0=p,
                                                            scalar1=sc),
                                        waits=[("pe", tpe)], nosync=True)
                        p1_free[slot] = td
                        last_pe = tpe
                return last_pe

            q_last_pe = proj_fm(0, wq_ready, q_sb, 2.0 ** -5)
            wv_ready = load_weight(wv, 0, False, q_last_pe, dma_extra=dx)
            k_last_pe = proj_fm(1, wk_ready, k_sb, None)
            wu_ready = load_weight(wu, 1, False, k_last_pe, dma_extra=dx)

            v_last_pe = 0
            gi = 0
            for tc in range(NTC):
                for og in range(2):
                    slot = gi % 4
                    gi += 1
                    ps = sc_ps[slot // 2][:, 512 * (slot % 2):512 * (slot % 2) + 512]
                    for kc in range(KC):
                        waits = []
                        if kc == 0:
                            waits = (list(wv_ready) + list(xf_ready[1]) +
                                     [("pool", setup_pool), ("dve", p1_free[slot])])
                        tpe = S.emit("tensor", lambda te, p=ps, xo=2048 * kc + 128 * tc,
                                     wo=D * kc + 512 * og, st_=(kc == 0), sp=(kc == KC - 1):
                                     te.matmul(p, x_fmb[:, xo:xo + 128],
                                               wslot[0][:, wo:wo + 512],
                                               start=st_, stop=sp),
                                     waits=waits)
                    v_last_pe = tpe
                    td = S.emit("vector", lambda v, p=ps, do=1024 * tc + 512 * og:
                                v.tensor_copy(out=v_aug[:, do:do + 512], in_=p),
                                waits=[("pe", tpe)], nosync=True)
                    p1_free[slot] = td
            w8_ready = load_weight(w8, 0, True, v_last_pe, dma_extra=dx)

            q_ready = S.t["dve"]
            first_att = [True]

            def p3_micros(tg_, tc):
                par = tg_ % 2
                r = tc % 2
                st = {}

                def u_half(og, half, extra_waits):
                    tpe = 0
                    for kc in range(4 * half, 4 * half + 4):
                        waits = list(extra_waits) if kc == 4 * half else []
                        tpe = S.emit("tensor", lambda te, oo=512 * kc + 128 * (tc % 4),
                                     wo=D * kc + 512 * og, p=par,
                                     st_=(kc == 0), sp=(kc == KC - 1):
                                     te.matmul(p3_ps[:], outn[p][:, oo:oo + 128],
                                               wslot[1][:, wo:wo + 512],
                                               start=st_, stop=sp),
                                     waits=waits)
                    return tpe

                def xres_fetch(sq, tc_):
                    rr = tc_ % 2
                    t = S.emit("sync", lambda s: s.dma_start(
                        out=xres[rr][:], in_=x_tm[sq, 128 * tc_:128 * (tc_ + 1), :]),
                        waits=[("dve", xres_free[rr]), borrow_free[0]],
                        sem=f"xr{rr}")
                    xres_tl[(sq, tc_)] = t
                    return t

                def m1():
                    st["tl"] = xres_tl.get((seq, tc)) or xres_fetch(seq, tc)
                    u_half(0, 0, [("dve", bank7[0]), ("dve", outn_done[par])]
                           + list(wu_ready))

                def m1b():
                    tpe = u_half(0, 1, [])
                    outn_rd[par] = tpe
                    st["ev0"] = S.emit("vector", lambda v:
                                       v.tensor_tensor(out=t1_v[:, 0:512], in0=p3_ps[:],
                                                       in1=bu_sb[:, 0:512], op=ALU.add),
                                       waits=[("pe", tpe), setup_ld], nosync=True)

                def m2():
                    u_half(1, 0, [("dve", st["ev0"])])

                def m2b():
                    tpe = u_half(1, 1, [])
                    outn_rd[par] = tpe
                    st["ev1"] = S.emit("vector", lambda v:
                                       v.tensor_tensor(out=t1_v[:, 512:1024], in0=p3_ps[:],
                                                       in1=bu_sb[:, 512:1024], op=ALU.add),
                                       waits=[("pe", tpe)], nosync=True)
                    tz = S.emit("vector", lambda v: v.tensor_tensor(
                        out=z1_v[:], in0=t1_v[:], in1=xres[r][:], op=ALU.add),
                        waits=[(f"xr{r}", st["tl"])])
                    xres_free[r] = tz

                def m3():
                    S.emit("vector", lambda v: v.bn_stats(out=bns1[r][:, 0:6],
                                                          in_=z1_v[:, 0:512]))
                    S.emit("vector", lambda v: v.bn_stats(out=bns1[r][:, 6:12],
                                                          in_=z1_v[:, 512:1024]))
                    S.emit("vector", lambda v: v.bn_aggr(out=mv1[r][:], in_=bns1[r][:]))
                    st["tv1"] = S.emit("vector", lambda v: v.tensor_scalar_add(
                        out=rin1[r][:], in0=mv1[r][:, 1:2], scalar1=EPS_LN),
                        waits=[("act", rin1_free[r])])

                def m3b():
                    rin1_free[r] = S.emit("scalar", lambda sc: sc.activation(
                        out=rln1[r][:], in_=rin1[r][:], func=AF.Ln),
                        waits=[("dve", st["tv1"])], nosync=True)
                    tex = S.emit("scalar", lambda sc: sc.activation(
                        out=rstd1[r][:], in_=rln1[r][:], func=AF.Exp, scale=-0.5),
                        waits=[("dve", rstd1_free[r])])
                    ty = S.emit("vector", lambda v: v.tensor_scalar(
                        out=y_v[:], in0=z1_v[:], scalar1=mv1[r][:, 0:1],
                        scalar2=rstd1[r][:],
                        op0=ALU.subtract, op1=ALU.mult), waits=[("act", tex)])
                    rstd1_free[r] = ty
                    S.emit("vector", lambda v: v.reduce_max(
                        out=amx[r][:], in_=y_v[:], axis=AX.X, apply_absolute_value=True))
                    S.emit("vector", lambda v: v.tensor_scalar(
                        out=xsv[r][:], in0=amx[r][:], scalar1=EPS_Q, scalar2=1.0 / 127.0,
                        op0=ALU.max, op1=ALU.mult))
                    S.emit("vector", lambda v: v.reciprocal(out=rxs[r][:], in_=xsv[r][:]))

                def m4():
                    S.emit("vector", lambda v: v.tensor_scalar_mul(
                        out=tq_v[:], in0=y_v[:], scalar1=rxs[r][:]))
                    S.emit("vector", lambda v: v.tensor_copy(out=xqi_v, in_=tq_v[:]))
                    st["xq"] = S.emit("vector", lambda v: v.tensor_copy(
                        out=xqb_v, in_=xqi_v))
                    st["ybb"] = S.emit("vector", lambda v: v.tensor_tensor(
                        out=ybb_v[:], in0=y_v[:], in1=bia_sb[:], op=ALU.add))

                def m5():
                    for jj in range(4):
                        waits = []
                        if jj == 0:
                            waits = [("dve", st["ev1"]), ("dve", st["xq"]), setup_ld]
                        S.emit("tensor", lambda te, o=128 * jj:
                               te.transpose(xqT_ps[:, o:o + 128],
                                            xqb_v[:, o:o + 128], ident_sb[:]),
                               waits=waits)

                def m5b():
                    tpe = 0
                    for jj in range(4, KC):
                        tpe = S.emit("tensor", lambda te, o=128 * jj:
                                     te.transpose(xqT_ps[:, o:o + 128],
                                                  xqb_v[:, o:o + 128], ident_sb[:]))
                    st["xqT"] = S.emit("vector", lambda v: v.tensor_copy(
                        out=xqT_v, in_=xqT_ps[:]), waits=[("pe", tpe)], nosync=True)

                def f_half(og, half, extra_waits):
                    tpe = 0
                    for kc in range(4 * half, 4 * half + 4):
                        waits = list(extra_waits) if kc == 4 * half else []
                        tpe = S.emit("tensor", lambda te, xo=128 * kc,
                                     wo=D * kc + 512 * og,
                                     st_=(kc == 0), sp=(kc == KC - 1):
                                     te.matmul(p3_ps[:], xqT_v[:, xo:xo + 128],
                                               wslot[0][:, wo:wo + 512],
                                               start=st_, stop=sp),
                                     waits=waits)
                    return tpe

                def m6():
                    f_half(0, 0, [("dve", st["xqT"])] + list(w8_ready))

                def m6b():
                    tpe = f_half(0, 1, [])
                    st["dqA"] = S.emit("vector", lambda v: v.scalar_tensor_tensor(
                        out=f2_v[:, 0:512], in0=p3_ps[:], scalar=xsv[r][:],
                        in1=wsc_sb[:, 0:512], op0=ALU.mult, op1=ALU.mult),
                        waits=[("pe", tpe)], nosync=True)

                def m7():
                    f_half(1, 0, [("dve", st["dqA"])])

                def m7a():
                    tpe = f_half(1, 1, [])
                    dq = S.emit("vector", lambda v: v.scalar_tensor_tensor(
                        out=f2_v[:, 512:1024], in0=p3_ps[:], scalar=xsv[r][:],
                        in1=wsc_sb[:, 512:1024], op0=ALU.mult, op1=ALU.mult),
                        waits=[("pe", tpe)], nosync=True)
                    bank7[0] = dq
                    S.emit("vector", lambda v: v.tensor_tensor(
                        out=z2_v[:], in0=f2_v[:], in1=ybb_v[:], op=ALU.add))
                    S.emit("vector", lambda v: v.bn_stats(out=bns2[r][:, 0:6],
                                                          in_=z2_v[:, 0:512]))
                    S.emit("vector", lambda v: v.bn_stats(out=bns2[r][:, 6:12],
                                                          in_=z2_v[:, 512:1024]))
                    S.emit("vector", lambda v: v.bn_aggr(out=mv2[r][:], in_=bns2[r][:]))
                    st["tv2"] = S.emit("vector", lambda v: v.tensor_scalar_add(
                        out=rin2[r][:], in0=mv2[r][:, 1:2], scalar1=EPS_LN),
                        waits=[("act", rin2_free[r])])

                def m7b():
                    rin2_free[r] = S.emit("scalar", lambda sc: sc.activation(
                        out=rln2[r][:], in_=rin2[r][:], func=AF.Ln),
                        waits=[("dve", st["tv2"])], nosync=True)
                    tex = S.emit("scalar", lambda sc: sc.activation(
                        out=rstd2[r][:], in_=rln2[r][:], func=AF.Exp, scale=-0.5),
                        waits=[("dve", rstd2_free[r])])
                    tno = S.emit("vector", lambda v: v.tensor_scalar(
                        out=outt_v[:], in0=z2_v[:], scalar1=mv2[r][:, 0:1],
                        scalar2=rstd2[r][:],
                        op0=ALU.subtract, op1=ALU.mult),
                        waits=[("act", tex), ("st", last_store[0])])
                    rstd2_free[r] = tno
                    if tc + 1 < NTC:
                        xres_fetch(seq, tc + 1)
                    ts = S.emit("sync", lambda s, sq=seq: s.dma_start(
                        out=out[sq, 128 * tc:128 * (tc + 1), :], in_=outt_v[:]),
                        waits=[("dve", tno)], sem="st")
                    last_store[0] = ts

                gap = lambda: None
                return [m1, m1b, m2, m2b, m3, gap, gap, m3b, m4, m5, m5b, m6,
                        m6b, m7, m7a, gap, gap, m7b]

            outn_done = [0, 0]
            p3_queue = []

            def emit_attv(j, sc, tg, buf, te_tick):
                hA, hB = 2 * j, 2 * j + 1
                par = j % 2
                op = out_pair[par]
                wA = [("act", te_tick)]
                if sc == 0:
                    wA.append(("dve", op_free[par]))
                S.emit("tensor", lambda te, vo=1024 * sc + 64 * hA, b=buf,
                       st_=(sc == 0), sp=(sc == NSC - 1):
                       te.matmul(op[0:64, :], v_aug[:, vo:vo + 64],
                                 exp_sb[b][:, 0:512], start=st_, stop=sp,
                                 tile_position=(0, 0), skip_group_check=True),
                       waits=wA)
                S.emit("tensor", lambda te, vo=1024 * sc + 64 * hB, b=buf,
                       st_=(sc == 0), sp=(sc == NSC - 1):
                       te.matmul(op[64:128, :], v_aug[:, vo:vo + 64],
                                 exp_sb[b][:, 512:1024], start=st_, stop=sp,
                                 tile_position=(0, 64), skip_group_check=True))
                wD = [("act", den_free[0])] if sc == 0 else []
                S.emit("tensor", lambda te, b=buf, st_=(sc == 0), sp=(sc == NSC - 1):
                       te.matmul(den_ps[0:64, :], ones128[:, 0:64],
                                 exp_sb[b][:, 0:512], start=st_, stop=sp,
                                 tile_position=(0, 0), skip_group_check=True),
                       waits=wD)
                tdn = S.emit("tensor", lambda te, b=buf, st_=(sc == 0), sp=(sc == NSC - 1):
                             te.matmul(den_ps[64:128, :], ones128[:, 0:64],
                                       exp_sb[b][:, 512:1024], start=st_, stop=sp,
                                       tile_position=(0, 64), skip_group_check=True))
                attv_tick[buf] = tdn
                return tdn

            def emit_epi_a(tdn):
                tln = S.emit("scalar", lambda s_: s_.activation(
                    out=dtmp[:], in_=den_ps[:], func=AF.Ln),
                    waits=[("pe", tdn), ("act", dtmp_free[0]), borrow_free[0]],
                    nosync=True)
                den_free[0] = tln
                return tln

            def emit_epi_b(j, tg, tln):
                hA, hB = 2 * j, 2 * j + 1
                par = tg % 2
                opp = out_pair[j % 2]
                tre = S.emit("scalar", lambda s_: s_.activation(
                    out=rdp[:], in_=dtmp[:], func=AF.Exp, scale=-1.0),
                    waits=[("dve", rdp_free[0]), borrow_free[0]])
                dtmp_free[0] = tre
                ttA = S.emit("vector", lambda v, h=hA, p=par: v.tensor_tensor(
                    out=outn[p][(h % 2) * 64:(h % 2) * 64 + 64,
                                512 * (h // 2):512 * (h // 2) + 512],
                    in0=opp[0:64, :], in1=rdp[0:64, :], op=ALU.mult),
                    waits=[("pe", outn_rd[par]), ("act", tre)], nosync=True)
                ttB = S.emit("vector", lambda v, h=hB, p=par: v.tensor_tensor(
                    out=outn[p][(h % 2) * 64:(h % 2) * 64 + 64,
                                512 * (h // 2):512 * (h // 2) + 512],
                    in0=opp[64:128, :], in1=rdp[64:128, :], op=ALU.mult), nosync=True)
                op_free[j % 2] = ttB
                rdp_free[0] = ttB
                if j == 7:
                    outn_done[par] = ttB

            pending = []
            pending_b = [None]

            for tg in range(NTG):
                for j in range(8):
                    for sc in range(NSC):
                        sbuf_i = astep[0] % 2
                        ebuf = astep[0] % 5
                        astep[0] += 1
                        wsc = [("act", exp_tick[sbuf_i])]
                        if first_att[0]:
                            wsc.append(("dve", q_ready))
                            first_att[0] = False
                        S.emit("tensor", lambda te, ko=2048 * j + 128 * sc,
                               qo=2048 * j + 512 * tg, b=sbuf_i:
                               te.matmul(sc_ps[b][:, 0:512], k_sb[0:64, ko:ko + 128],
                                         q_sb[0:64, qo:qo + 512], start=True, stop=True),
                               waits=wsc)
                        t2m = S.emit("tensor", lambda te, ko=2048 * j + 128 * sc,
                                     qo=2048 * j + 512 * tg, b=sbuf_i:
                                     te.matmul(sc_ps[b][:, 512:1024],
                                               k_sb[64:128, ko:ko + 128],
                                               q_sb[64:128, qo:qo + 512],
                                               start=True, stop=True))
                        te_ = S.emit("scalar", lambda s_, b=sbuf_i, e=ebuf: s_.activation(
                            out=exp_sb[e][:], in_=sc_ps[b][:], func=AF.Exp),
                            waits=[("pe", t2m), ("pe", attv_tick[ebuf])], nosync=True)
                        exp_tick[sbuf_i] = te_
                        if sc == 2 and pending_b[0] is not None:
                            pending_b[0]()
                            pending_b[0] = None
                        if sc % 2 == 1 and (j > 0 or sc >= 3) and p3_queue:
                            p3_queue.pop(0)()
                        if len(pending) >= 2:
                            pending.pop(0)()
                        def deferred(j_=j, sc_=sc, tg_=tg, e_=ebuf, tk=te_):
                            tdn = emit_attv(j_, sc_, tg_, e_, tk)
                            if sc_ == NSC - 1:
                                tre = emit_epi_a(tdn)
                                pending_b[0] = (lambda jj=j_, tt=tg_, tr=tre:
                                                emit_epi_b(jj, tt, tr))
                        pending.append(deferred)

                for i in range(4):
                    p3_queue.extend(p3_micros(tg, 4 * tg + i))

            while pending:
                pending.pop(0)()
            if pending_b[0] is not None:
                pending_b[0]()
                pending_b[0] = None
            while p3_queue:
                p3_queue.pop(0)()

        with nc.Block() as block:
            def replay(eng_name, eng):
                own = ENG_SEM[eng_name]
                last_wait = {}
                self_serial = eng_name in ("vector", "scalar", "gpsimd")
                k = 0
                for builder, waits, sname, inc, nosync in S.ops[eng_name]:
                    if self_serial and k > 0 and not nosync:
                        eng.wait_ge(sem[own], k)
                    k += inc
                    for w_sem, val in waits:
                        if w_sem == own:
                            continue
                        if last_wait.get(w_sem, -1) >= val:
                            continue
                        eng.wait_ge(sem[w_sem], val)
                        last_wait[w_sem] = val
                    builder(eng).then_inc(sem[sname], inc)

            total_st = S.t["st"]

            @block.sync
            def _(se):
                replay("sync", se)
                se.wait_ge(sem["st"], total_st)

            @block.tensor
            def _(te):
                replay("tensor", te)

            @block.vector
            def _(v):
                replay("vector", v)

            @block.scalar
            def _(sc):
                replay("scalar", sc)

            @block.gpsimd
            def _(g):
                replay("gpsimd", g)

    return nc


_NC_CACHE = {}


def _get_nc():
    if "nc" not in _NC_CACHE:
        _NC_CACHE["nc"] = build_program()
    return _NC_CACHE["nc"]


def kernel(L0_attn_x, L0_attn_Wq, L0_attn_Wk, L0_attn_Wv, L0_attn_Wu, L0_attn_bu,
           L0_ffn_weight_q, L0_ffn_weight_scale, L0_ffn_bias):
    x = np.asarray(L0_attn_x, np.float32)
    wqT = np.ascontiguousarray(np.asarray(L0_attn_Wq, np.float32).T)
    wkT = np.ascontiguousarray(np.asarray(L0_attn_Wk, np.float32).T)
    wvT = np.ascontiguousarray(np.asarray(L0_attn_Wv, np.float32).T)
    wuT = np.ascontiguousarray(np.asarray(L0_attn_Wu, np.float32).T)
    w8T = np.ascontiguousarray(np.asarray(L0_ffn_weight_q, np.int8).T)
    bu_b = np.ascontiguousarray(np.broadcast_to(
        np.asarray(L0_attn_bu, np.float32)[None, :], (128, D)))
    ws_b = np.ascontiguousarray(np.broadcast_to(
        np.asarray(L0_ffn_weight_scale, np.float32)[None, :], (128, D)))
    bias_b = np.ascontiguousarray(np.broadcast_to(
        np.asarray(L0_ffn_bias, np.float32)[None, :], (128, D)))
    ident = np.eye(128).astype(bf)

    nc = _get_nc()
    in_maps = []
    for c in range(8):
        xc = x[2 * c:2 * c + 2]
        in_maps.append({
            "x_fm": np.ascontiguousarray(xc.transpose(0, 2, 1)),
            "x_tm": np.ascontiguousarray(xc),
            "wq": wqT, "wk": wkT, "wv": wvT, "wu": wuT, "w8": w8T,
            "bu_b": bu_b, "ws_b": ws_b, "bias_b": bias_b, "ident": ident,
        })
    res = run_bass_kernel_spmd(nc, in_maps, core_ids=list(range(8)))
    return np.concatenate([np.asarray(r["out"], np.float32) for r in res.results],
                          axis=0)


if __name__ == "__main__":
    print("building program...")
    nc = build_program()
    print("instructions:", nc.n_instructions())


# version 23
# speedup vs baseline: 1.0841x; 1.0053x over previous
import numpy as np
import ml_dtypes
import concourse.bass as bass
import concourse.mybir as mybir
from concourse.bass_utils import run_bass_kernel_spmd
from contextlib import ExitStack

F32 = mybir.dt.float32
BF16 = mybir.dt.bfloat16
I8 = mybir.dt.int8
I32 = mybir.dt.int32
AF = mybir.ActivationFunctionType
ALU = mybir.AluOpType
AX = mybir.AxisListType
bf = ml_dtypes.bfloat16

D = 1024
T = 2048
NSEQ = 2
H = 16
HD = 64
KC = 8
NTG = 4
NTC = 16
NSC = 16
EPS_LN = 1e-5
EPS_Q = 1e-6
VW = 1040

ENG_SEM = {"sync": None, "tensor": "pe", "vector": "dve", "scalar": "act", "gpsimd": "pool"}


class Sched:

    SEMS = ("set", "ws0", "ws1", "xs0", "xs1", "xr0", "xr1", "b0", "b1",
            "b2", "b3", "w0", "w1", "p0", "p1", "st", "pe", "act", "dve",
            "pool")

    def __init__(self):
        self.ops = {e: [] for e in ("sync", "tensor", "vector", "scalar", "gpsimd")}
        self.t = {s: 0 for s in self.SEMS}

    def emit(self, eng, builder, waits=(), sem=None, inc=None, nosync=False):
        if sem is None:
            sem = ENG_SEM[eng]
        if inc is None:
            inc = 1 if sem in ("pe", "act", "dve", "pool") else 16
        w = [(s, v) for (s, v) in waits if v and v > 0]
        self.ops[eng].append((builder, w, sem, inc, nosync))
        self.t[sem] += inc
        return self.t[sem]


def build_program():
    nc = bass.Bass("TRN2", target_bir_lowering=False, debug=False)

    x_fm = nc.dram_tensor("x_fm", [NSEQ, D, T], F32, kind="ExternalInput").ap()
    x_tm = nc.dram_tensor("x_tm", [NSEQ, T, D], F32, kind="ExternalInput").ap()
    wq = nc.dram_tensor("wq", [D, D], F32, kind="ExternalInput").ap()
    wk = nc.dram_tensor("wk", [D, D], F32, kind="ExternalInput").ap()
    wv = nc.dram_tensor("wv", [D, D], F32, kind="ExternalInput").ap()
    wu = nc.dram_tensor("wu", [D, D], F32, kind="ExternalInput").ap()
    w8 = nc.dram_tensor("w8", [D, D], I8, kind="ExternalInput").ap()
    bu_b = nc.dram_tensor("bu_b", [128, D], F32, kind="ExternalInput").ap()
    ws_b = nc.dram_tensor("ws_b", [128, D], F32, kind="ExternalInput").ap()
    bias_b = nc.dram_tensor("bias_b", [128, D], F32, kind="ExternalInput").ap()
    ident = nc.dram_tensor("ident", [128, 128], BF16, kind="ExternalInput").ap()
    out = nc.dram_tensor("out", [NSEQ, T, D], F32, kind="ExternalOutput").ap()
    wspill = nc.dram_tensor("wspill", [2, 128, KC * D], BF16, kind="Internal").ap()

    with ExitStack() as ctx:
        E = ctx.enter_context

        wslot = [E(nc.sbuf_tensor(f"wslot{i}", [128, KC * D], BF16)) for i in range(2)]
        stage = [E(nc.sbuf_tensor(f"stage{i}", [128, 1024], F32)) for i in range(2)]
        xres = [E(nc.sbuf_tensor(f"xres{i}", [128, D], F32)) for i in range(2)]
        arena = E(nc.sbuf_tensor("arena", [128, 16384], BF16))
        q_sb = E(nc.sbuf_tensor("q_sb", [128, KC * T], BF16))
        k_sb = E(nc.sbuf_tensor("k_sb", [128, KC * T], BF16))
        v_aug = E(nc.sbuf_tensor("v_aug", [128, NSC * 1024], BF16))
        exp_sb = [E(nc.sbuf_tensor(f"exp{i}", [128, 1024], BF16)) for i in range(5)]
        dtmp = E(nc.sbuf_tensor("dtmp", [128, 512], F32))
        out_t = E(nc.sbuf_tensor("out_t", [128, D], F32))
        rdp = E(nc.sbuf_tensor("rdp", [128, 512], F32))
        ones128 = E(nc.sbuf_tensor("ones128", [128, 64], BF16))
        ident_sb = E(nc.sbuf_tensor("ident_sb", [128, 128], BF16))
        bu_sb = E(nc.sbuf_tensor("bu_sb", [128, D], F32))
        wsc_sb = E(nc.sbuf_tensor("wsc_sb", [128, D], F32))
        bia_sb = E(nc.sbuf_tensor("bia_sb", [128, D], F32))
        bns1 = [E(nc.sbuf_tensor(f"bns1_{i}", [128, 12], F32)) for i in range(2)]
        mv1 = [E(nc.sbuf_tensor(f"mv1_{i}", [128, 2], F32)) for i in range(2)]
        rin1 = [E(nc.sbuf_tensor(f"rin1_{i}", [128, 1], F32)) for i in range(2)]
        rln1 = [E(nc.sbuf_tensor(f"rln1_{i}", [128, 1], F32)) for i in range(2)]
        rstd1 = [E(nc.sbuf_tensor(f"rstd1_{i}", [128, 1], F32)) for i in range(2)]
        bns2 = [E(nc.sbuf_tensor(f"bns2_{i}", [128, 12], F32)) for i in range(2)]
        mv2 = [E(nc.sbuf_tensor(f"mv2_{i}", [128, 2], F32)) for i in range(2)]
        rin2 = [E(nc.sbuf_tensor(f"rin2_{i}", [128, 1], F32)) for i in range(2)]
        rln2 = [E(nc.sbuf_tensor(f"rln2_{i}", [128, 1], F32)) for i in range(2)]
        rstd2 = [E(nc.sbuf_tensor(f"rstd2_{i}", [128, 1], F32)) for i in range(2)]
        amx = [E(nc.sbuf_tensor(f"amx{i}", [128, 1], F32)) for i in range(2)]
        xsv = [E(nc.sbuf_tensor(f"xsv{i}", [128, 1], F32)) for i in range(2)]
        rxs = [E(nc.sbuf_tensor(f"rxs{i}", [128, 1], F32)) for i in range(2)]

        sc_ps = [E(nc.psum_tensor(f"sc_ps{i}", [128, 1024], F32)) for i in range(2)]
        out_pair = [E(nc.psum_tensor(f"op{i}", [128, 512], F32)) for i in range(2)]
        den_ps = E(nc.psum_tensor("den_ps", [128, 512], F32))
        p3_ps = E(nc.psum_tensor("p3_ps", [128, 512], F32))

        sem = {n: E(nc.semaphore(n)) for n in Sched.SEMS}

        a = arena.ap()
        af = a.bitcast(F32)
        ai = a.bitcast(I32)
        x_fmb = a
        outn = [a[:, 0:4096], a[:, 4096:8192]]
        t1_v = af[:, 4096:5120]
        xqi_v = ai[:, 4096:5120]
        f2_v = af[:, 4096:5120]
        z1_v = af[:, 5120:6144]
        tq_v = z1_v
        ybb_v = z1_v
        outt_v = out_t.ap()
        y_v = af[:, 6144:7168]
        z2_v = y_v
        xqb_v = a[:, 14336:15360]
        xqT_v = a[:, 15360:16384]
        xqT_ps = p3_ps.ap().bitcast(BF16)

        S = Sched()

        S.emit("gpsimd", lambda g: g.memset(ones128[:], 1.0))
        setup_pool = S.t["pool"]

        for dst, src in ((bu_sb, bu_b), (wsc_sb, ws_b), (bia_sb, bias_b),
                         (ident_sb, ident)):
            t_ = S.emit("sync", lambda s, d=dst, sr=src: s.dma_start(out=d[:], in_=sr),
                        sem="set")
        setup_ld = ("set", t_)

        ring8 = [stage[0][:, 0:512], stage[0][:, 512:1024],
                 stage[1][:, 0:512], stage[1][:, 512:1024],
                 xres[0][:, 0:512], xres[0][:, 512:1024],
                 xres[1][:, 0:512], xres[1][:, 512:1024],
                 dtmp[:], rdp[:]]
        RING8_SEMS = ["ws0", "ws1", "xs0", "xs1", "xr0", "xr1", "b0", "b1",
                      "b2", "b3"]
        stage_free = [("pool", 0), ("pool", 0)]
        ring_free = [("pool", 0)] * 10
        stage_n = [0]
        borrow_free = [("pool", 0)]
        xres_free = [0, 0]
        p1_free = [0, 0, 0, 0]
        exp_tick = [0, 0]
        attv_tick = [0] * 5
        op_free = [0, 0]
        den_free = [0]
        rdp_free = [0]
        dtmp_free = [0]
        bank7 = [0]
        last_store = [0]
        outn_rd = [0, 0]
        rin1_free = [0, 0]
        rstd1_free = [0, 0]
        rin2_free = [0, 0]
        rstd2_free = [0, 0]
        astep = [0]

        def load_weight(w_ap, slot, is_int8, war_pe, extra=(), dma_extra=()):
            last = 0
            nch = KC if is_int8 else 2 * KC
            for ch in range(nch):
                if is_int8:
                    si = stage_n[0] % 4
                    if si >= 2:
                        si = si % 2
                    stage_n[0] += 4 - stage_n[0] % 2 if False else 1
                    src = w_ap[128 * ch:128 * (ch + 1), :]
                    dst = stage[si].ap().bitcast(I8)[:, 0:D]
                    fslot = 2 * si
                    sm = RING8_SEMS[fslot]
                    so, w_ = D * ch, D
                    wfree = [ring_free[2 * si], ring_free[2 * si + 1]]
                else:
                    fslot = stage_n[0] % 10
                    stage_n[0] += 1
                    kc, hh = ch // 2, ch % 2
                    src = w_ap[128 * kc:128 * (kc + 1), 512 * hh:512 * (hh + 1)]
                    dst = ring8[fslot]
                    sm = RING8_SEMS[fslot]
                    so, w_ = D * kc + 512 * hh, 512
                    wfree = [ring_free[fslot]]
                tl = S.emit("sync", lambda s, d=dst, sr=src: s.dma_start(out=d, in_=sr),
                            waits=wfree + list(dma_extra), sem=sm)
                waits = [(sm, tl)]
                if ch == 0:
                    if war_pe:
                        waits.append(("pe", war_pe))
                    waits.extend(extra)
                tp = S.emit("vector", lambda g, sl=slot, o=so, ww=w_, s_=dst:
                            g.tensor_copy(out=wslot[sl][:, o:o + ww], in_=s_),
                            waits=waits, nosync=True)
                tk = ("dve", tp)
                if is_int8:
                    ring_free[2 * si] = tk
                    ring_free[2 * si + 1] = tk
                else:
                    ring_free[fslot] = tk
                    if fslot >= 4:
                        borrow_free[0] = tk
                last = tp
            return [("dve", last)]

        def load_x_half(seq, hh, extra=(), dma_extra=()):
            last = 0
            for pc in range(KC):
                for qq in range(2):
                    fslot = stage_n[0] % 10
                    stage_n[0] += 1
                    c0 = 1024 * hh + 512 * qq
                    src = x_fm[seq, 128 * pc:128 * (pc + 1), c0:c0 + 512]
                    d_ap = ring8[fslot]
                    sm = RING8_SEMS[fslot]
                    tl = S.emit("sync", lambda s, d=d_ap, sr=src:
                                s.dma_start(out=d, in_=sr),
                                waits=[ring_free[fslot]] + list(dma_extra), sem=sm)
                    w = ([(sm, tl)]
                         + (list(extra) if pc == 0 and qq == 0 else []))
                    tp = S.emit("gpsimd", lambda g, o=2048 * pc + c0, s_=d_ap:
                                g.tensor_copy(out=x_fmb[:, o:o + 512], in_=s_),
                                waits=w, nosync=True)
                    ring_free[fslot] = ("pool", tp)
                    if fslot >= 4:
                        borrow_free[0] = ("pool", tp)
                    last = tp
            return [("pool", last)]

        for seq in range(NSEQ):
            bar = [] if seq == 0 else [("pe", S.t["pe"]), ("dve", S.t["dve"]),
                                       ("act", S.t["act"]), ("st", S.t["st"])]

            dx = bar if seq == 1 else ()
            if seq == 0:
                wq_ready = load_weight(wq, 0, False, None, extra=bar)
                wk_ready = load_weight(wk, 1, False, None)
                S.emit("sync", lambda s_: s_.dma_start(
                    out=wspill[0], in_=wslot[0][:]), waits=wq_ready, sem="p0")
                S.emit("sync", lambda s_: s_.dma_start(
                    out=wspill[1], in_=wslot[1][:]), waits=wk_ready, sem="p1")
            else:
                t = S.emit("sync", lambda s_: s_.dma_start(
                    out=wslot[0][:], in_=wspill[0]),
                    waits=[("p0", 16)] + list(bar), sem="w0")
                wq_ready = [("w0", t)]
                t = S.emit("sync", lambda s_: s_.dma_start(
                    out=wslot[1][:], in_=wspill[1]),
                    waits=[("p1", 16)] + list(bar), sem="w1")
                wk_ready = [("w1", t)]
            xf_ready = [load_x_half(seq, 0, dma_extra=dx),
                        load_x_half(seq, 1, dma_extra=dx)]

            def proj_fm(w_slot, w_ready, dest_sb, scale):
                last_pe = 0
                gi = [0]
                for tg in range(NTG):
                    for oc in range(KC):
                        slot = gi[0] % 4
                        gi[0] += 1
                        ps = sc_ps[slot // 2][:, 512 * (slot % 2):512 * (slot % 2) + 512]
                        for kc in range(KC):
                            waits = []
                            if kc == 0:
                                waits = list(w_ready) + list(xf_ready[tg // 2]) + [
                                    ("dve", p1_free[slot])]
                            tpe = S.emit("tensor", lambda te, p=ps,
                                         wo=D * kc + 128 * oc, xo=2048 * kc + 512 * tg,
                                         sl=w_slot, st_=(kc == 0), sp=(kc == KC - 1):
                                         te.matmul(p, wslot[sl][:, wo:wo + 128],
                                                   x_fmb[:, xo:xo + 512],
                                                   start=st_, stop=sp),
                                         waits=waits)
                        if scale is None:
                            td = S.emit("vector", lambda v, p=ps, d=dest_sb,
                                        o=2048 * oc + 512 * tg:
                                        v.tensor_copy(out=d[:, o:o + 512], in_=p),
                                        waits=[("pe", tpe)], nosync=True)
                        else:
                            td = S.emit("vector", lambda v, p=ps, d=dest_sb,
                                        o=2048 * oc + 512 * tg, sc=scale:
                                        v.tensor_scalar_mul(out=d[:, o:o + 512], in0=p,
                                                            scalar1=sc),
                                        waits=[("pe", tpe)], nosync=True)
                        p1_free[slot] = td
                        last_pe = tpe
                return last_pe

            q_last_pe = proj_fm(0, wq_ready, q_sb, 2.0 ** -5)
            wv_ready = load_weight(wv, 0, False, q_last_pe, dma_extra=dx,
                                   extra=([("p0", 16)] if seq == 0 else []))
            k_last_pe = proj_fm(1, wk_ready, k_sb, None)
            wu_ready = load_weight(wu, 1, False, k_last_pe, dma_extra=dx,
                                   extra=([("p1", 16)] if seq == 0 else []))

            v_last_pe = 0
            gi = 0
            for tc in range(NTC):
                for og in range(2):
                    slot = gi % 4
                    gi += 1
                    ps = sc_ps[slot // 2][:, 512 * (slot % 2):512 * (slot % 2) + 512]
                    for kc in range(KC):
                        waits = []
                        if kc == 0:
                            waits = (list(wv_ready) + list(xf_ready[1]) +
                                     [("pool", setup_pool), ("dve", p1_free[slot])])
                        tpe = S.emit("tensor", lambda te, p=ps, xo=2048 * kc + 128 * tc,
                                     wo=D * kc + 512 * og, st_=(kc == 0), sp=(kc == KC - 1):
                                     te.matmul(p, x_fmb[:, xo:xo + 128],
                                               wslot[0][:, wo:wo + 512],
                                               start=st_, stop=sp),
                                     waits=waits)
                    v_last_pe = tpe
                    td = S.emit("vector", lambda v, p=ps, do=1024 * tc + 512 * og:
                                v.tensor_copy(out=v_aug[:, do:do + 512], in_=p),
                                waits=[("pe", tpe)], nosync=True)
                    p1_free[slot] = td
            w8_ready = load_weight(w8, 0, True, v_last_pe, dma_extra=dx)

            q_ready = S.t["dve"]
            first_att = [True]

            def p3_micros(tg_, tc):
                par = tg_ % 2
                r = tc % 2
                st = {}

                def u_half(og, half, extra_waits):
                    tpe = 0
                    for kc in range(4 * half, 4 * half + 4):
                        waits = list(extra_waits) if kc == 4 * half else []
                        tpe = S.emit("tensor", lambda te, oo=512 * kc + 128 * (tc % 4),
                                     wo=D * kc + 512 * og, p=par,
                                     st_=(kc == 0), sp=(kc == KC - 1):
                                     te.matmul(p3_ps[:], outn[p][:, oo:oo + 128],
                                               wslot[1][:, wo:wo + 512],
                                               start=st_, stop=sp),
                                     waits=waits)
                    return tpe

                def m1():
                    st["tl"] = S.emit("sync", lambda s, sq=seq: s.dma_start(
                        out=xres[r][:], in_=x_tm[sq, 128 * tc:128 * (tc + 1), :]),
                        waits=[("dve", xres_free[r]), borrow_free[0]],
                        sem=f"xr{r}")
                    u_half(0, 0, [("dve", bank7[0]), ("dve", outn_done[par])]
                           + list(wu_ready))

                def m1b():
                    tpe = u_half(0, 1, [])
                    outn_rd[par] = tpe
                    st["ev0"] = S.emit("vector", lambda v:
                                       v.tensor_tensor(out=t1_v[:, 0:512], in0=p3_ps[:],
                                                       in1=bu_sb[:, 0:512], op=ALU.add),
                                       waits=[("pe", tpe), setup_ld], nosync=True)

                def m2():
                    u_half(1, 0, [("dve", st["ev0"])])

                def m2b():
                    tpe = u_half(1, 1, [])
                    outn_rd[par] = tpe
                    st["ev1"] = S.emit("vector", lambda v:
                                       v.tensor_tensor(out=t1_v[:, 512:1024], in0=p3_ps[:],
                                                       in1=bu_sb[:, 512:1024], op=ALU.add),
                                       waits=[("pe", tpe)], nosync=True)
                    tz = S.emit("vector", lambda v: v.tensor_tensor(
                        out=z1_v[:], in0=t1_v[:], in1=xres[r][:], op=ALU.add),
                        waits=[(f"xr{r}", st["tl"])])
                    xres_free[r] = tz

                def m3():
                    S.emit("vector", lambda v: v.bn_stats(out=bns1[r][:, 0:6],
                                                          in_=z1_v[:, 0:512]))
                    S.emit("vector", lambda v: v.bn_stats(out=bns1[r][:, 6:12],
                                                          in_=z1_v[:, 512:1024]))
                    S.emit("vector", lambda v: v.bn_aggr(out=mv1[r][:], in_=bns1[r][:]))
                    st["tv1"] = S.emit("vector", lambda v: v.tensor_scalar_add(
                        out=rin1[r][:], in0=mv1[r][:, 1:2], scalar1=EPS_LN),
                        waits=[("act", rin1_free[r])])

                def m3b():
                    rin1_free[r] = S.emit("scalar", lambda sc: sc.activation(
                        out=rln1[r][:], in_=rin1[r][:], func=AF.Ln),
                        waits=[("dve", st["tv1"])], nosync=True)
                    tex = S.emit("scalar", lambda sc: sc.activation(
                        out=rstd1[r][:], in_=rln1[r][:], func=AF.Exp, scale=-0.5),
                        waits=[("dve", rstd1_free[r])])
                    ty = S.emit("vector", lambda v: v.tensor_scalar(
                        out=y_v[:], in0=z1_v[:], scalar1=mv1[r][:, 0:1],
                        scalar2=rstd1[r][:],
                        op0=ALU.subtract, op1=ALU.mult), waits=[("act", tex)])
                    rstd1_free[r] = ty
                    S.emit("vector", lambda v: v.reduce_max(
                        out=amx[r][:], in_=y_v[:], axis=AX.X, apply_absolute_value=True))
                    S.emit("vector", lambda v: v.tensor_scalar(
                        out=xsv[r][:], in0=amx[r][:], scalar1=EPS_Q, scalar2=1.0 / 127.0,
                        op0=ALU.max, op1=ALU.mult))
                    S.emit("vector", lambda v: v.reciprocal(out=rxs[r][:], in_=xsv[r][:]))

                def m4():
                    S.emit("vector", lambda v: v.tensor_scalar_mul(
                        out=tq_v[:], in0=y_v[:], scalar1=rxs[r][:]))
                    S.emit("vector", lambda v: v.tensor_copy(out=xqi_v, in_=tq_v[:]))
                    st["xq"] = S.emit("vector", lambda v: v.tensor_copy(
                        out=xqb_v, in_=xqi_v))
                    st["ybb"] = S.emit("vector", lambda v: v.tensor_tensor(
                        out=ybb_v[:], in0=y_v[:], in1=bia_sb[:], op=ALU.add))

                def m5():
                    for jj in range(4):
                        waits = []
                        if jj == 0:
                            waits = [("dve", st["ev1"]), ("dve", st["xq"]), setup_ld]
                        S.emit("tensor", lambda te, o=128 * jj:
                               te.transpose(xqT_ps[:, o:o + 128],
                                            xqb_v[:, o:o + 128], ident_sb[:]),
                               waits=waits)

                def m5b():
                    tpe = 0
                    for jj in range(4, KC):
                        tpe = S.emit("tensor", lambda te, o=128 * jj:
                                     te.transpose(xqT_ps[:, o:o + 128],
                                                  xqb_v[:, o:o + 128], ident_sb[:]))
                    st["xqT"] = S.emit("vector", lambda v: v.tensor_copy(
                        out=xqT_v, in_=xqT_ps[:]), waits=[("pe", tpe)], nosync=True)

                def f_half(og, half, extra_waits):
                    tpe = 0
                    for kc in range(4 * half, 4 * half + 4):
                        waits = list(extra_waits) if kc == 4 * half else []
                        tpe = S.emit("tensor", lambda te, xo=128 * kc,
                                     wo=D * kc + 512 * og,
                                     st_=(kc == 0), sp=(kc == KC - 1):
                                     te.matmul(p3_ps[:], xqT_v[:, xo:xo + 128],
                                               wslot[0][:, wo:wo + 512],
                                               start=st_, stop=sp),
                                     waits=waits)
                    return tpe

                def m6():
                    f_half(0, 0, [("dve", st["xqT"])] + list(w8_ready))

                def m6b():
                    tpe = f_half(0, 1, [])
                    st["dqA"] = S.emit("vector", lambda v: v.scalar_tensor_tensor(
                        out=f2_v[:, 0:512], in0=p3_ps[:], scalar=xsv[r][:],
                        in1=wsc_sb[:, 0:512], op0=ALU.mult, op1=ALU.mult),
                        waits=[("pe", tpe)], nosync=True)

                def m7():
                    f_half(1, 0, [("dve", st["dqA"])])

                def m7a():
                    tpe = f_half(1, 1, [])
                    dq = S.emit("vector", lambda v: v.scalar_tensor_tensor(
                        out=f2_v[:, 512:1024], in0=p3_ps[:], scalar=xsv[r][:],
                        in1=wsc_sb[:, 512:1024], op0=ALU.mult, op1=ALU.mult),
                        waits=[("pe", tpe)], nosync=True)
                    bank7[0] = dq
                    S.emit("vector", lambda v: v.tensor_tensor(
                        out=z2_v[:], in0=f2_v[:], in1=ybb_v[:], op=ALU.add))
                    S.emit("vector", lambda v: v.bn_stats(out=bns2[r][:, 0:6],
                                                          in_=z2_v[:, 0:512]))
                    S.emit("vector", lambda v: v.bn_stats(out=bns2[r][:, 6:12],
                                                          in_=z2_v[:, 512:1024]))
                    S.emit("vector", lambda v: v.bn_aggr(out=mv2[r][:], in_=bns2[r][:]))
                    st["tv2"] = S.emit("vector", lambda v: v.tensor_scalar_add(
                        out=rin2[r][:], in0=mv2[r][:, 1:2], scalar1=EPS_LN),
                        waits=[("act", rin2_free[r])])

                def m7b():
                    rin2_free[r] = S.emit("scalar", lambda sc: sc.activation(
                        out=rln2[r][:], in_=rin2[r][:], func=AF.Ln),
                        waits=[("dve", st["tv2"])], nosync=True)
                    tex = S.emit("scalar", lambda sc: sc.activation(
                        out=rstd2[r][:], in_=rln2[r][:], func=AF.Exp, scale=-0.5),
                        waits=[("dve", rstd2_free[r])])
                    tno = S.emit("vector", lambda v: v.tensor_scalar(
                        out=outt_v[:], in0=z2_v[:], scalar1=mv2[r][:, 0:1],
                        scalar2=rstd2[r][:],
                        op0=ALU.subtract, op1=ALU.mult),
                        waits=[("act", tex), ("st", last_store[0])])
                    rstd2_free[r] = tno
                    ts = S.emit("sync", lambda s, sq=seq: s.dma_start(
                        out=out[sq, 128 * tc:128 * (tc + 1), :], in_=outt_v[:]),
                        waits=[("dve", tno)], sem="st")
                    last_store[0] = ts

                gap = lambda: None
                return [m1, m1b, m2, m2b, m3, gap, gap, m3b, m4, m5, m5b, m6,
                        m6b, m7, m7a, gap, gap, m7b]

            outn_done = [0, 0]
            p3_queue = []

            def emit_attv(j, sc, tg, buf, te_tick):
                hA, hB = 2 * j, 2 * j + 1
                par = j % 2
                op = out_pair[par]
                wA = [("act", te_tick)]
                if sc == 0:
                    wA.append(("dve", op_free[par]))
                S.emit("tensor", lambda te, vo=1024 * sc + 64 * hA, b=buf,
                       st_=(sc == 0), sp=(sc == NSC - 1):
                       te.matmul(op[0:64, :], v_aug[:, vo:vo + 64],
                                 exp_sb[b][:, 0:512], start=st_, stop=sp,
                                 tile_position=(0, 0), skip_group_check=True),
                       waits=wA)
                S.emit("tensor", lambda te, vo=1024 * sc + 64 * hB, b=buf,
                       st_=(sc == 0), sp=(sc == NSC - 1):
                       te.matmul(op[64:128, :], v_aug[:, vo:vo + 64],
                                 exp_sb[b][:, 512:1024], start=st_, stop=sp,
                                 tile_position=(0, 64), skip_group_check=True))
                wD = [("act", den_free[0])] if sc == 0 else []
                S.emit("tensor", lambda te, b=buf, st_=(sc == 0), sp=(sc == NSC - 1):
                       te.matmul(den_ps[0:64, :], ones128[:, 0:64],
                                 exp_sb[b][:, 0:512], start=st_, stop=sp,
                                 tile_position=(0, 0), skip_group_check=True),
                       waits=wD)
                tdn = S.emit("tensor", lambda te, b=buf, st_=(sc == 0), sp=(sc == NSC - 1):
                             te.matmul(den_ps[64:128, :], ones128[:, 0:64],
                                       exp_sb[b][:, 512:1024], start=st_, stop=sp,
                                       tile_position=(0, 64), skip_group_check=True))
                attv_tick[buf] = tdn
                return tdn

            def emit_epi_a(tdn):
                tln = S.emit("scalar", lambda s_: s_.activation(
                    out=dtmp[:], in_=den_ps[:], func=AF.Ln),
                    waits=[("pe", tdn), ("act", dtmp_free[0]), borrow_free[0]],
                    nosync=True)
                den_free[0] = tln
                return tln

            def emit_epi_b(j, tg, tln):
                hA, hB = 2 * j, 2 * j + 1
                par = tg % 2
                opp = out_pair[j % 2]
                tre = S.emit("scalar", lambda s_: s_.activation(
                    out=rdp[:], in_=dtmp[:], func=AF.Exp, scale=-1.0),
                    waits=[("dve", rdp_free[0]), borrow_free[0]])
                dtmp_free[0] = tre
                ttA = S.emit("vector", lambda v, h=hA, p=par: v.tensor_tensor(
                    out=outn[p][(h % 2) * 64:(h % 2) * 64 + 64,
                                512 * (h // 2):512 * (h // 2) + 512],
                    in0=opp[0:64, :], in1=rdp[0:64, :], op=ALU.mult),
                    waits=[("pe", outn_rd[par]), ("act", tre)], nosync=True)
                ttB = S.emit("vector", lambda v, h=hB, p=par: v.tensor_tensor(
                    out=outn[p][(h % 2) * 64:(h % 2) * 64 + 64,
                                512 * (h // 2):512 * (h // 2) + 512],
                    in0=opp[64:128, :], in1=rdp[64:128, :], op=ALU.mult), nosync=True)
                op_free[j % 2] = ttB
                rdp_free[0] = ttB
                if j == 7:
                    outn_done[par] = ttB

            pending = []
            pending_b = [None]

            for tg in range(NTG):
                for j in range(8):
                    for sc in range(NSC):
                        sbuf_i = astep[0] % 2
                        ebuf = astep[0] % 5
                        astep[0] += 1
                        wsc = [("act", exp_tick[sbuf_i])]
                        if first_att[0]:
                            wsc.append(("dve", q_ready))
                            first_att[0] = False
                        S.emit("tensor", lambda te, ko=2048 * j + 128 * sc,
                               qo=2048 * j + 512 * tg, b=sbuf_i:
                               te.matmul(sc_ps[b][:, 0:512], k_sb[0:64, ko:ko + 128],
                                         q_sb[0:64, qo:qo + 512], start=True, stop=True),
                               waits=wsc)
                        t2m = S.emit("tensor", lambda te, ko=2048 * j + 128 * sc,
                                     qo=2048 * j + 512 * tg, b=sbuf_i:
                                     te.matmul(sc_ps[b][:, 512:1024],
                                               k_sb[64:128, ko:ko + 128],
                                               q_sb[64:128, qo:qo + 512],
                                               start=True, stop=True))
                        te_ = S.emit("scalar", lambda s_, b=sbuf_i, e=ebuf: s_.activation(
                            out=exp_sb[e][:], in_=sc_ps[b][:], func=AF.Exp),
                            waits=[("pe", t2m), ("pe", attv_tick[ebuf])], nosync=True)
                        exp_tick[sbuf_i] = te_
                        if sc == 2 and pending_b[0] is not None:
                            pending_b[0]()
                            pending_b[0] = None
                        if sc % 2 == 1 and (j > 0 or sc >= 3) and p3_queue:
                            p3_queue.pop(0)()
                        if len(pending) >= 2:
                            pending.pop(0)()
                        def deferred(j_=j, sc_=sc, tg_=tg, e_=ebuf, tk=te_):
                            tdn = emit_attv(j_, sc_, tg_, e_, tk)
                            if sc_ == NSC - 1:
                                tre = emit_epi_a(tdn)
                                pending_b[0] = (lambda jj=j_, tt=tg_, tr=tre:
                                                emit_epi_b(jj, tt, tr))
                        pending.append(deferred)

                for i in range(4):
                    p3_queue.extend(p3_micros(tg, 4 * tg + i))

            while pending:
                pending.pop(0)()
            if pending_b[0] is not None:
                pending_b[0]()
                pending_b[0] = None
            while p3_queue:
                p3_queue.pop(0)()

        with nc.Block() as block:
            def replay(eng_name, eng):
                own = ENG_SEM[eng_name]
                last_wait = {}
                self_serial = eng_name in ("vector", "scalar", "gpsimd")
                k = 0
                for builder, waits, sname, inc, nosync in S.ops[eng_name]:
                    if self_serial and k > 0 and not nosync:
                        eng.wait_ge(sem[own], k)
                    k += inc
                    for w_sem, val in waits:
                        if w_sem == own:
                            continue
                        if last_wait.get(w_sem, -1) >= val:
                            continue
                        eng.wait_ge(sem[w_sem], val)
                        last_wait[w_sem] = val
                    builder(eng).then_inc(sem[sname], inc)

            total_st = S.t["st"]

            @block.sync
            def _(se):
                replay("sync", se)
                se.wait_ge(sem["st"], total_st)

            @block.tensor
            def _(te):
                replay("tensor", te)

            @block.vector
            def _(v):
                replay("vector", v)

            @block.scalar
            def _(sc):
                replay("scalar", sc)

            @block.gpsimd
            def _(g):
                replay("gpsimd", g)

    return nc


_NC_CACHE = {}


def _get_nc():
    if "nc" not in _NC_CACHE:
        _NC_CACHE["nc"] = build_program()
    return _NC_CACHE["nc"]


def kernel(L0_attn_x, L0_attn_Wq, L0_attn_Wk, L0_attn_Wv, L0_attn_Wu, L0_attn_bu,
           L0_ffn_weight_q, L0_ffn_weight_scale, L0_ffn_bias):
    x = np.asarray(L0_attn_x, np.float32)
    wqT = np.ascontiguousarray(np.asarray(L0_attn_Wq, np.float32).T)
    wkT = np.ascontiguousarray(np.asarray(L0_attn_Wk, np.float32).T)
    wvT = np.ascontiguousarray(np.asarray(L0_attn_Wv, np.float32).T)
    wuT = np.ascontiguousarray(np.asarray(L0_attn_Wu, np.float32).T)
    w8T = np.ascontiguousarray(np.asarray(L0_ffn_weight_q, np.int8).T)
    bu_b = np.ascontiguousarray(np.broadcast_to(
        np.asarray(L0_attn_bu, np.float32)[None, :], (128, D)))
    ws_b = np.ascontiguousarray(np.broadcast_to(
        np.asarray(L0_ffn_weight_scale, np.float32)[None, :], (128, D)))
    bias_b = np.ascontiguousarray(np.broadcast_to(
        np.asarray(L0_ffn_bias, np.float32)[None, :], (128, D)))
    ident = np.eye(128).astype(bf)

    nc = _get_nc()
    in_maps = []
    for c in range(8):
        xc = x[2 * c:2 * c + 2]
        in_maps.append({
            "x_fm": np.ascontiguousarray(xc.transpose(0, 2, 1)),
            "x_tm": np.ascontiguousarray(xc),
            "wq": wqT, "wk": wkT, "wv": wvT, "wu": wuT, "w8": w8T,
            "bu_b": bu_b, "ws_b": ws_b, "bias_b": bias_b, "ident": ident,
        })
    res = run_bass_kernel_spmd(nc, in_maps, core_ids=list(range(8)))
    return np.concatenate([np.asarray(r["out"], np.float32) for r in res.results],
                          axis=0)


if __name__ == "__main__":
    print("building program...")
    nc = build_program()
    print("instructions:", nc.n_instructions())
